# version 07
# baseline (speedup 1.0000x reference)
import contextlib
import numpy as np
import concourse.bass as bass
import concourse.mybir as mybir
from concourse.bass_utils import run_bass_kernel_spmd

F32 = mybir.dt.float32
BF16 = mybir.dt.bfloat16
U32 = mybir.dt.uint32
AF = mybir.ActivationFunctionType
ALU = mybir.AluOpType
AX = mybir.AxisListType

NCORE = 8
D = 1024
PD = 256
FF = 4096
E = 8
T = 4096
NBT = 8
NT = 32
A = 512
NST = 15
SLOTS = NST * A
EPS_LN = 1e-5
EPS_Z = 1e-8


class S:
    def __init__(self, nc, es, name):
        self.h = es.enter_context(nc.semaphore(name))
        self.n = 0

    def dma(self, instr):
        instr.then_inc(self.h, 16)
        self.n += 16
        return self.n

    def inc(self, instr):
        instr.then_inc(self.h, 1)
        self.n += 1
        return self.n

    def wait(self, eng, n=None):
        n = self.n if n is None else n
        if n > 0:
            eng.wait_ge(self.h, n)


def build():
    nc = bass.Bass("TRN2", target_bir_lowering=False)
    _uid = [0]

    def U(name):
        _uid[0] += 1
        return f"{name}_{_uid[0]}"

    te, ve, se, ge, sy = nc.tensor, nc.vector, nc.scalar, nc.gpsimd, nc.sync

    x_d = nc.dram_tensor("x", [T, D], F32, kind="ExternalInput").ap()
    wqT_d = nc.dram_tensor("wqT", [D, D], F32, kind="ExternalInput").ap()
    wkT_d = nc.dram_tensor("wkT", [D, D], F32, kind="ExternalInput").ap()
    wvT_d = nc.dram_tensor("wvT", [D, D], F32, kind="ExternalInput").ap()
    proj_d = nc.dram_tensor("proj", [D, PD], F32, kind="ExternalInput").ap()
    cT_d = nc.dram_tensor("cT", [D, E], F32, kind="ExternalInput").ap()
    gam_d = nc.dram_tensor("gam", [E, D], F32, kind="ExternalInput").ap()
    bet_d = nc.dram_tensor("bet", [E, D], F32, kind="ExternalInput").ap()
    b1r_d = nc.dram_tensor("b1r", [E, FF], F32, kind="ExternalInput").ap()
    b2r_d = nc.dram_tensor("b2r", [E, D], F32, kind="ExternalInput").ap()
    w1T_d = nc.dram_tensor("w1T", [E, D, FF], F32, kind="ExternalInput").ap()
    w2T_d = nc.dram_tensor("w2T", [E, FF, D], F32, kind="ExternalInput").ap()

    out_d = nc.dram_tensor("out", [T, D], F32, kind="ExternalOutput").ap()
    x1_d = nc.dram_tensor("x1d", [T, D], F32, kind="Internal").ap()
    fn_tbl = nc.dram_tensor("fn_tbl", [SLOTS, D], BF16, kind="Internal").ap()
    y2_tbl = nc.dram_tensor("y2_tbl", [SLOTS, D], F32, kind="Internal").ap()
    b2scr_d = nc.dram_tensor("b2scr", [1, NST * D], F32, kind="Internal").ap()
    b2scr_2d = b2scr_d.rearrange("o (s d) -> (o s) d", d=D)

    w1_lin = w1T_d.rearrange("e d f -> (e d) f")
    w2_lin = w2T_d.rearrange("e f d -> (e f) d")

    es = contextlib.ExitStack()
    with es:
        xs1 = [S(nc, es, f"xs1_{i}") for i in range(2)]
        xs2 = [S(nc, es, f"xs2_{i}") for i in range(2)]
        uT = [S(nc, es, f"uT_{i}") for i in range(2)]
        fT = [S(nc, es, f"fT_{i}") for i in range(2)]
        hT = S(nc, es, "hT")
        qWI = S(nc, es, "qWI")
        qW1 = S(nc, es, "qW1")
        qW2 = S(nc, es, "qW2")
        qIs = [S(nc, es, f"qIs_{i}") for i in range(2)]
        qIg = [S(nc, es, f"qIg_{i}") for i in range(2)]
        qX1 = [S(nc, es, f"qX1_{i}") for i in range(2)]
        qXC = [S(nc, es, f"qXC_{i}") for i in range(2)]
        qY2 = S(nc, es, "qY2")
        qO = [S(nc, es, f"qO_{i}") for i in range(2)]
        qB = S(nc, es, "qB")
        qZ = S(nc, es, "qZ")
        qG = S(nc, es, "qG")
        cP = S(nc, es, "cP")
        cV = S(nc, es, "cV")
        cS = S(nc, es, "cS")
        cG = S(nc, es, "cG")

        def V(instr):
            return cV.inc(instr)

        def Sc(instr):
            return cS.inc(instr)

        def MM(out, lhsT, rhs, start, stop, tag=False):
            ins = te.matmul(out, lhsT, rhs, start=start, stop=stop)
            if tag:
                return cP.inc(ins)
            return None

        ones_bf = es.enter_context(nc.sbuf_tensor("ones_bf", [128, 128], BF16))
        ones_f = es.enter_context(nc.sbuf_tensor("ones_f", [1, 128], F32))
        tri_bf = es.enter_context(nc.sbuf_tensor("tri_bf", [128, 128], BF16))
        iota8 = es.enter_context(nc.sbuf_tensor("iota8", [128, 8], F32))
        mione = es.enter_context(nc.sbuf_tensor("mione", [128, 8], F32))
        base_row = es.enter_context(nc.sbuf_tensor("base_row", [1, 8], F32))
        ct_sb = es.enter_context(nc.sbuf_tensor("ct_sb", [128, 8, E], BF16))
        csum_row = es.enter_context(nc.sbuf_tensor("csum_row", [1, E], F32))
        csumB = es.enter_context(nc.sbuf_tensor("csumB", [128, E], F32))
        alpha_sb = es.enter_context(nc.sbuf_tensor("alpha_sb", [128, NT], F32))
        dest_sb = es.enter_context(nc.sbuf_tensor("dest_sb", [128, NT], U32))
        s3_sb = es.enter_context(nc.sbuf_tensor("s3_sb", [128, NT], F32))
        b3_sb = es.enter_context(nc.sbuf_tensor("b3_sb", [128, NT], F32))
        z_sb = es.enter_context(nc.sbuf_tensor("z_sb", [128, NT], F32))
        pos_all = es.enter_context(nc.sbuf_tensor("pos_all", [128, NT], F32))
        oh_all = es.enter_context(nc.sbuf_tensor("oh_all", [128, NT, E], F32))
        iota_p = es.enter_context(nc.sbuf_tensor("iota_p", [128, 1], F32))
        iota_pm1 = es.enter_context(nc.sbuf_tensor("iota_pm1", [128, 1], F32))
        iota_pdc = es.enter_context(nc.sbuf_tensor("iota_pdc", [128, 8], F32))
        iota_pfc = es.enter_context(nc.sbuf_tensor("iota_pfc", [128, 32], F32))
        thrT = es.enter_context(nc.sbuf_tensor("thrT", [8, 16], F32))
        I8 = es.enter_context(nc.sbuf_tensor("I8", [8, 8], F32))
        subD8 = es.enter_context(nc.sbuf_tensor("subD8", [8, 8], F32))
        D8 = es.enter_context(nc.sbuf_tensor("D8", [8, 8], F32))
        sU8 = es.enter_context(nc.sbuf_tensor("sU8", [8, 8], F32))
        ones8 = es.enter_context(nc.sbuf_tensor("ones8", [8, 1], F32))
        g_all = es.enter_context(nc.sbuf_tensor("g_all", [128, 8, NST], F32))
        bt_all = es.enter_context(nc.sbuf_tensor("bt_all", [128, 8, NST], F32))
        b1_all = es.enter_context(nc.sbuf_tensor("b1_all", [128, 32, NST], F32))
        w1off_u = es.enter_context(nc.sbuf_tensor("w1off_u", [128, 8, NST],
                                                  U32))
        w2off_u = es.enter_context(nc.sbuf_tensor("w2off_u", [128, 32, NST],
                                                  U32))

        blk = es.enter_context(nc.Block())

        cG.inc(ge.memset(ones_bf[:], 1.0))
        cG.inc(ge.memset(ones_f[:], 1.0))
        cG.wait(ge)
        cG.inc(ge.affine_select(tri_bf[:], ones_bf[:], [[1, 128]], ALU.is_ge,
                                0.0, base=0, channel_multiplier=-1))
        cG.inc(ge.iota(iota8[:], [[1, 8]], base=0, channel_multiplier=0,
                       allow_small_or_imprecise_dtypes=True))
        cG.inc(ge.memset(mione[:], -1.0))
        cG.inc(ge.memset(ones8[:], 1.0))
        cG.inc(ge.iota(iota_p[:], [[0, 1]], base=0, channel_multiplier=1,
                       allow_small_or_imprecise_dtypes=True))
        cG.inc(ge.iota(iota_pm1[:], [[0, 1]], base=-1, channel_multiplier=1,
                       allow_small_or_imprecise_dtypes=True))
        cG.inc(ge.iota(iota_pdc[:], [[128, 8]], base=0, channel_multiplier=1,
                       allow_small_or_imprecise_dtypes=True))
        cG.inc(ge.iota(iota_pfc[:], [[128, 32]], base=0, channel_multiplier=1,
                       allow_small_or_imprecise_dtypes=True))
        cG.inc(ge.iota(thrT[:], [[A, 16]], base=0, channel_multiplier=0,
                       allow_small_or_imprecise_dtypes=True))
        m_init_g = cG.inc(ge.memset(base_row[:], 0.0))
        cG.wait(ve, m_init_g)
        V(ve.tensor_scalar(I8[:], iota8[0:8, :], iota_p[0:8, 0:1], None,
                           ALU.is_equal))
        V(ve.tensor_scalar(subD8[:], iota8[0:8, :], iota_pm1[0:8, 0:1], None,
                           ALU.is_equal))
        V(ve.tensor_scalar(sU8[:], iota8[0:8, :], iota_p[0:8, 0:1], None,
                           ALU.is_gt))
        cV.wait(ve)
        m_init_v = V(ve.tensor_tensor(D8[:], I8[:], subD8[:], ALU.subtract))
        qWI.dma(ge.dma_start(ct_sb[:], cT_d.rearrange("(c p) e -> p c e",
                                                      p=128)))

        ab = contextlib.ExitStack()
        with ab:
            qp_fm = ab.enter_context(nc.sbuf_tensor("qp_fm", [128, 2, T],
                                                    BF16))
            kv_bf = ab.enter_context(nc.sbuf_tensor("kv_bf", [128, 2, 2, D],
                                                    BF16))
            bn6 = ab.enter_context(nc.sbuf_tensor("bn6", [128, 12], F32))
            mv2 = ab.enter_context(nc.sbuf_tensor("mv2", [128, 2], F32))
            varp = ab.enter_context(nc.sbuf_tensor("varp", [128, 1], F32))
            lnv = ab.enter_context(nc.sbuf_tensor("lnv", [128, 1], F32))
            rstd_t = ab.enter_context(nc.sbuf_tensor("rstd_t", [128, 1], F32))
            nmr_t = ab.enter_context(nc.sbuf_tensor("nmr_t", [128, 1], F32))

            stx = {"se": 0, "ve": 0}

            def stats(src, rstd_out=None, nmr_out=None, mu_out=None):
                rstd_out = rstd_t[:] if rstd_out is None else rstd_out
                nmr_out = nmr_t[:] if nmr_out is None else nmr_out
                cS.wait(ve, stx["se"])
                cV.wait(ve)
                V(ve.bn_stats(bn6[:, 0:6], src[:, 0:512]))
                m1 = V(ve.bn_stats(bn6[:, 6:12], src[:, 512:1024]))
                cV.wait(ve, m1)
                m2 = V(ve.bn_aggr(mv2[:], bn6[:]))
                cV.wait(ve, m2)
                m3 = V(ve.tensor_scalar(varp[:], mv2[:, 1:2], EPS_LN, None,
                                        ALU.add))
                if mu_out is not None:
                    V(ve.tensor_copy(mu_out, mv2[:, 0:1]))
                cV.wait(se, m3)
                m4 = Sc(se.activation(lnv[:], varp[:], AF.Ln))
                cS.wait(se, m4)
                m5 = Sc(se.activation(rstd_out, lnv[:], AF.Exp, scale=-0.5))
                stx["se"] = m5
                cS.wait(ve, m5)
                m6 = V(ve.tensor_scalar(nmr_out, mv2[:, 0:1], rstd_out, -1.0,
                                        ALU.mult, ALU.mult))
                stx["ve"] = m6
                return m5, m6

            at = contextlib.ExitStack()
            with at:
                wq_sb = at.enter_context(nc.sbuf_tensor("wq_sb", [128, 8, D],
                                                        BF16))
                wk_sb = at.enter_context(nc.sbuf_tensor("wk_sb", [128, 8, D],
                                                        BF16))
                wv_sb = at.enter_context(nc.sbuf_tensor("wv_sb", [128, 8, D],
                                                        BF16))
                projn = at.enter_context(nc.sbuf_tensor("projn", [128, 8, PD],
                                                        BF16))
                x_sb = [at.enter_context(nc.sbuf_tensor(f"x_sb{i}", [128, D],
                                                        F32))
                        for i in range(2)]
                u_tm = at.enter_context(nc.sbuf_tensor("u_tm", [128, 2, D],
                                                       BF16))
                u_fm = [at.enter_context(nc.sbuf_tensor(f"u_fm{i}",
                                                        [128, 8, 512], BF16))
                        for i in range(2)]
                qev = at.enter_context(nc.sbuf_tensor("qev", [128, 2, 512],
                                                      BF16))
                kev = at.enter_context(nc.sbuf_tensor("kev", [128, 8, 512],
                                                      BF16))
                kp_tm = at.enter_context(nc.sbuf_tensor("kp_tm", [128, 4, PD],
                                                        BF16))
                v_tm = [at.enter_context(nc.sbuf_tensor(f"v_tm{i}",
                                                        [128, 4, D], BF16))
                        for i in range(2)]
                zt_sb = at.enter_context(nc.sbuf_tensor("zt_sb",
                                                        [128, 4, 1024], BF16))
                kv_sb = at.enter_context(nc.sbuf_tensor("kv_sb", [128, 2, D],
                                                        F32))
                el_t = at.enter_context(nc.sbuf_tensor("el_t", [128, 1024],
                                                       F32))
                el_e = at.enter_context(nc.sbuf_tensor("el_e", [128, 1024],
                                                       F32))
                el_r = at.enter_context(nc.sbuf_tensor("el_r", [128, 1024],
                                                       F32))

                m_zt = cG.inc(ge.memset(zt_sb[:], 0.0))
                cG.wait(sy, m_zt)
                for i in range(SLOTS // 512):
                    qZ.dma(sy.dma_start(
                        fn_tbl[i * 512:(i + 1) * 512, :].rearrange(
                            "(p a) f -> p a f", p=128),
                        zt_sb[:]))
                qWI.dma(ge.dma_start(wq_sb[:],
                                     wqT_d.rearrange("(c p) e -> p c e",
                                                     p=128)))
                qWI.dma(ge.dma_start(wk_sb[:],
                                     wkT_d.rearrange("(c p) e -> p c e",
                                                     p=128)))
                qWI.dma(ge.dma_start(wv_sb[:],
                                     wvT_d.rearrange("(c p) e -> p c e",
                                                     p=128)))
                qWI.dma(ge.dma_start(projn[:],
                                     proj_d.rearrange("(c p) q -> p c q",
                                                      p=128)))

                with contextlib.ExitStack() as ps:
                    csum_ps = ps.enter_context(nc.psum_tensor("csum_ps",
                                                              [1, E], F32))
                    csumB_ps = ps.enter_context(nc.psum_tensor("csumB_ps",
                                                               [128, E], F32))
                    qWI.wait(te)
                    cG.wait(te, m_init_g)
                    for c in range(8):
                        m1 = MM(csum_ps[:], ones_bf[:, 0:1], ct_sb[:, c, :],
                                start=(c == 0), stop=(c == 7), tag=(c == 7))
                    cP.wait(ve, m1)
                    m2 = V(ve.tensor_copy(csum_row[:], csum_ps[:]))
                    cV.wait(te, m2)
                    m3 = MM(csumB_ps[:], ones_f[:], csum_row[:], start=True,
                            stop=True, tag=True)
                    cP.wait(ve, m3)
                    m_csumB = V(ve.tensor_copy(csumB[:], csumB_ps[:]))

                m_x_use_s = [0, 0]
                m_x_use_v = [0, 0]
                m_ufm_use = [0, 0]
                m_kv_ready = [0, 0]
                m_z_ready = [0] * NBT
                m_qp_ready = [0] * NBT
                m_kvadd_done = 0
                m_z = 0
                m_prev_qpkp = 0
                m_prev_tile_pe = 0

                def load_x(g, buf):
                    cS.wait(sy, m_x_use_s[buf])
                    cV.wait(sy, m_x_use_v[buf])
                    return xs1[buf].dma(sy.dma_start(
                        x_sb[buf][:], x_d[g * 128:(g + 1) * 128, :]))

                m_x_ld = [0] * (4 * NBT)
                m_x_ld[0] = load_x(0, 0)

                for bt in range(NBT):
                    buf = bt % 2
                    seq = bt // 4
                    cP.wait(sy, m_ufm_use[buf])
                    m_tr = 0
                    for s in range(4):
                        g = bt * 4 + s
                        xb = g % 2
                        if g + 1 < 4 * NBT:
                            m_x_ld[g + 1] = load_x(g + 1, (g + 1) % 2)
                        xs1[xb].wait(ve, m_x_ld[g])
                        m_rstd, m_nmr = stats(x_sb[xb])
                        ub = s % 2
                        uT[buf].wait(se)
                        cV.wait(se, m_nmr)
                        xs1[xb].wait(se, m_x_ld[g])
                        m_u = Sc(se.activation(u_tm[:, ub, :], x_sb[xb][:],
                                               AF.Identity, bias=nmr_t[:],
                                               scale=rstd_t[:]))
                        m_x_use_s[xb] = m_u
                        m_x_use_v[xb] = cV.n
                        cS.wait(sy, m_u)
                        for dc in range(8):
                            m_tr = uT[buf].dma(sy.dma_start(
                                u_fm[buf][:, dc, s * 128:(s + 1) * 128],
                                u_tm[:, ub, dc * 128:(dc + 1) * 128],
                                transpose=True))
                    with contextlib.ExitStack() as ps:
                        q_ps = [ps.enter_context(nc.psum_tensor(U("q_ps"),
                                                                [128, 512],
                                                                F32))
                                for i in range(2)]
                        k_ps = [ps.enter_context(nc.psum_tensor(U("k_ps"),
                                                                [128, 512],
                                                                F32))
                                for i in range(2)]
                        qp_ps = [ps.enter_context(nc.psum_tensor(U("qp_ps"),
                                                                 [128, 512],
                                                                 F32))
                                 for i in range(2)]
                        kp_ps = ps.enter_context(nc.psum_tensor(U("kp_ps"),
                                                                [128, 4, PD],
                                                                F32))
                        uT[buf].wait(te, m_tr)
                        if bt >= 1:
                            cV.wait(te, m_kvadd_done)
                            cV.wait(te, m_z)
                        m_qev = [0, 0]
                        m_kev = [0, 0]
                        cP.wait(se, m_prev_qpkp)
                        for ec in range(8):
                            r = ec % 2
                            cS.wait(te, m_qev[r])
                            for dc in range(8):
                                m_qmm = MM(q_ps[r][:],
                                           wq_sb[:, dc, ec * 128:(ec + 1) * 128],
                                           u_fm[buf][:, dc, :],
                                           start=(dc == 0), stop=(dc == 7),
                                           tag=(dc == 7))
                            cS.wait(te, m_kev[r])
                            for dc in range(8):
                                m_kmm = MM(k_ps[r][:],
                                           wk_sb[:, dc, ec * 128:(ec + 1) * 128],
                                           u_fm[buf][:, dc, :],
                                           start=(dc == 0), stop=(dc == 7),
                                           tag=(dc == 7))
                            cP.wait(se, m_qmm)
                            m_qev[r] = Sc(se.activation(qev[:, r, :],
                                                        q_ps[r][:], AF.Copy))
                            cP.wait(se, m_kmm)
                            m_kev[r] = Sc(se.activation(kev[:, ec, :],
                                                        k_ps[r][:], AF.Copy))
                            cS.wait(te, m_qev[r])
                            for pc in range(2):
                                m_qpmm = MM(qp_ps[pc][:],
                                            projn[:, ec, pc * 128:(pc + 1) * 128],
                                            qev[:, r, :], start=(ec == 0),
                                            stop=(ec == 7),
                                            tag=(ec == 7 and pc == 1))
                        cS.wait(te, m_kev[0])
                        cS.wait(te, m_kev[1])
                        for s in range(4):
                            for ec in range(8):
                                m_qpkp = MM(kp_ps[:, s, :],
                                            kev[:, ec, s * 128:(s + 1) * 128],
                                            projn[:, ec, :], start=(ec == 0),
                                            stop=(ec == 7),
                                            tag=(s == 3 and ec == 7))
                        m_prev_qpkp = m_qpkp
                        cP.wait(ve, m_qpmm)
                        cP.wait(ve, m_prev_tile_pe)
                        qpv0 = qp_ps[0][:]
                        qpv1 = qp_ps[1][:]
                        V(ve.tensor_scalar(el_t[:, 0:512], qpv0, 0.0, None,
                                           ALU.min))
                        V(ve.tensor_scalar(el_t[:, 512:1024], qpv1, 0.0, None,
                                           ALU.min))
                        V(ve.tensor_scalar(el_r[:, 0:512], qpv0, 0.0, None,
                                           ALU.max))
                        m_e1 = V(ve.tensor_scalar(el_r[:, 512:1024], qpv1, 0.0,
                                                  None, ALU.max))
                        cV.wait(se, m_e1)
                        m_e2 = Sc(se.activation(el_e[:], el_t[:], AF.Exp))
                        cS.wait(ve, m_e2)
                        m_qp = 0
                        for pc in range(2):
                            m_qp = V(ve.scalar_tensor_tensor(
                                qp_fm[:, pc, bt * 512:(bt + 1) * 512],
                                el_r[:, pc * 512:(pc + 1) * 512], 0.0,
                                el_e[:, pc * 512:(pc + 1) * 512], ALU.add,
                                ALU.add))
                        m_qp_ready[bt] = m_qp
                        cP.wait(ve, m_qpkp)
                        kpv = kp_ps[:].rearrange("p s f -> p (s f)")
                        cV.wait(ve, m_qp)
                        V(ve.tensor_scalar(el_t[:], kpv, 0.0, None, ALU.min))
                        m_e3 = V(ve.tensor_scalar(el_r[:], kpv, 0.0, None,
                                                  ALU.max))
                        cV.wait(se, m_e3)
                        m_e4 = Sc(se.activation(el_e[:], el_t[:], AF.Exp))
                        cS.wait(ve, m_e4)
                        m_kp = V(ve.scalar_tensor_tensor(
                            kp_tm[:].rearrange("p s f -> p (s f)"),
                            el_r[:], 0.0, el_e[:], ALU.add, ALU.add))
                    with contextlib.ExitStack() as ps:
                        zs_ps = ps.enter_context(nc.psum_tensor(U("zs_ps"),
                                                                [128, 4], F32))
                        cV.wait(te, m_qp_ready[bt])
                        for s in range(4):
                            for pc in range(2):
                                m_zmm = MM(
                                    zs_ps[:, s:s + 1],
                                    qp_fm[:, pc,
                                          bt * 512 + s * 128:
                                          bt * 512 + (s + 1) * 128],
                                    ones_bf[:, 0:1], start=(pc == 0),
                                    stop=(pc == 1),
                                    tag=(s == 3 and pc == 1))
                        cP.wait(ve, m_zmm)
                        cS.wait(ve, m_e4)
                        m_zt2 = V(ve.tensor_scalar(el_t[:, 0:4], zs_ps[:],
                                                   EPS_Z, None, ALU.add))
                        cV.wait(ve, m_zt2)
                        m_z = V(ve.reciprocal(z_sb[:, bt * 4:(bt + 1) * 4],
                                              el_t[:, 0:4]))
                        m_z_ready[bt] = m_z
                    with contextlib.ExitStack() as ps:
                        v_ps = [ps.enter_context(nc.psum_tensor(U("v_ps"),
                                                                [128, D], F32))
                                for i in range(2)]
                        kv_ps = ps.enter_context(nc.psum_tensor(U("kv_ps"),
                                                                [128, 2, D],
                                                                F32))
                        cV.wait(te, m_z)
                        m_vev = [0, 0]
                        for s in range(4):
                            r = s % 2
                            cS.wait(te, m_vev[r])
                            for dc in range(8):
                                for h in range(2):
                                    m_vmm = MM(
                                        v_ps[r][:, h * 512:(h + 1) * 512],
                                        u_fm[buf][:, dc, s * 128:(s + 1) * 128],
                                        wv_sb[:, dc, h * 512:(h + 1) * 512],
                                        start=(dc == 0), stop=(dc == 7),
                                        tag=(dc == 7 and h == 1))
                            cP.wait(se, m_vmm)
                            m_vev[r] = Sc(se.activation(v_tm[buf][:, s, :],
                                                        v_ps[r][:], AF.Copy))
                            cS.wait(te, m_vev[r])
                            cV.wait(te, m_kp)
                            for pc in range(2):
                                for h in range(2):
                                    m_kvlast = MM(
                                        kv_ps[:, pc, h * 512:(h + 1) * 512],
                                        kp_tm[:, s, pc * 128:(pc + 1) * 128],
                                        v_tm[buf][:, s, h * 512:(h + 1) * 512],
                                        start=(s == 0), stop=(s == 3),
                                        tag=(s == 3 and pc == 1 and h == 1))
                        m_ufm_use[buf] = m_kvlast
                        m_prev_tile_pe = m_kvlast
                        cP.wait(ve, m_kvlast)
                        kvv = kv_ps[:].rearrange("p c d -> p (c d)")
                        kvs = kv_sb[:].rearrange("p c d -> p (c d)")
                        if bt % 4 == 0:
                            m_kvadd_done = V(ve.tensor_copy(kvs, kvv))
                        else:
                            cV.wait(ve, m_kvadd_done)
                            m_kvadd_done = V(ve.tensor_tensor(kvs, kvs, kvv,
                                                              ALU.add))
                        if bt % 4 == 3:
                            cV.wait(ve, m_kvadd_done)
                            m_kv_ready[seq] = V(ve.tensor_copy(
                                kv_bf[:, seq, :, :].rearrange(
                                    "p c d -> p (c d)"), kvs))

            mB_v, mB_s, mB_p = cV.n, cS.n, cP.n
            for eng in (sy, se, ve, ge):
                cV.wait(eng, mB_v)
                cS.wait(eng, mB_s)
                cP.wait(eng, mB_p)
                for qq in (xs1[0], xs1[1], uT[0], uT[1], qZ, qWI):
                    qq.wait(eng)

            p2 = contextlib.ExitStack()
            with p2:
                fn_res = p2.enter_context(nc.sbuf_tensor("fn_res",
                                                         [128, NT, D], BF16))
                x2_sb = [p2.enter_context(nc.sbuf_tensor(f"x2_sb{i}",
                                                         [128, D], F32))
                         for i in range(2)]
                ao_sb = p2.enter_context(nc.sbuf_tensor("ao_sb", [128, D],
                                                        F32))
                t2_sb = p2.enter_context(nc.sbuf_tensor("t2_sb", [128, D],
                                                        F32))
                x1_sb = [p2.enter_context(nc.sbuf_tensor(f"x1_sb{i}",
                                                         [128, D], F32))
                         for i in range(2)]
                fe_sb = p2.enter_context(nc.sbuf_tensor("fe_sb", [128, D],
                                                        F32))
                fnT_sb = [p2.enter_context(nc.sbuf_tensor(f"fnT_sb{i}",
                                                          [128, 8, 128],
                                                          BF16))
                          for i in range(2)]
                sc_sb = p2.enter_context(nc.sbuf_tensor("sc_sb", [128, E],
                                                        F32))
                sc2_sb = p2.enter_context(nc.sbuf_tensor("sc2_sb", [128, E],
                                                         F32))
                mx8 = p2.enter_context(nc.sbuf_tensor("mx8", [128, 8], F32))
                mi8 = p2.enter_context(nc.sbuf_tensor("mi8", [128, 8], U32))
                midx_f = p2.enter_context(nc.sbuf_tensor("midx_f", [128, 1],
                                                         F32))
                oh_bf = p2.enter_context(nc.sbuf_tensor("oh_bf", [128, 8],
                                                        BF16))
                ofsB = p2.enter_context(nc.sbuf_tensor("ofsB", [128, 8], F32))
                t1_sb = p2.enter_context(nc.sbuf_tensor("t1_sb", [128, 8],
                                                        F32))
                ea_sb = p2.enter_context(nc.sbuf_tensor("ea_sb", [128, 1],
                                                        F32))
                on1 = p2.enter_context(nc.sbuf_tensor("on1", [128, 1], F32))
                rstd2f = p2.enter_context(nc.sbuf_tensor("rstd2f", [128, 1],
                                                         F32))
                nmr2f = p2.enter_context(nc.sbuf_tensor("nmr2f", [128, 1],
                                                        F32))
                mu2f = p2.enter_context(nc.sbuf_tensor("mu2f", [128, 1], F32))
                ir2f = p2.enter_context(nc.sbuf_tensor("ir2f", [128, 1], F32))
                n_col_sb = p2.enter_context(nc.sbuf_tensor("n_col_sb", [8, 1],
                                                           F32))
                cmp8 = p2.enter_context(nc.sbuf_tensor("cmp8", [8, 8], F32))
                r8 = p2.enter_context(nc.sbuf_tensor("r8", [8, 1], F32))
                c_al_sb = p2.enter_context(nc.sbuf_tensor("c_al_sb", [8, 1],
                                                          F32))
                B_col_sb = p2.enter_context(nc.sbuf_tensor("B_col_sb", [8, 1],
                                                           F32))
                cmp2_sb = p2.enter_context(nc.sbuf_tensor("cmp2_sb", [8, NST],
                                                          F32))
                st_e_sb = p2.enter_context(nc.sbuf_tensor("st_e_sb", [1, NST],
                                                          F32))
                st_eB_sb = p2.enter_context(nc.sbuf_tensor("st_eB_sb",
                                                           [128, NST], F32))
                Brow32_sb = p2.enter_context(nc.sbuf_tensor("Brow32_sb",
                                                            [1, 256], F32))
                prod_sb = p2.enter_context(nc.sbuf_tensor("prod_sb",
                                                          [128, NT, E], F32))
                red_sb = p2.enter_context(nc.sbuf_tensor("red_sb", [128, NT],
                                                         F32))
                destf_sb = p2.enter_context(nc.sbuf_tensor("destf_sb",
                                                           [128, NT], F32))
                ohmat_sb = p2.enter_context(nc.sbuf_tensor("ohmat_sb",
                                                           [8, NST], F32))
                b2all_sb = p2.enter_context(nc.sbuf_tensor("b2all_sb",
                                                           [NST, D], F32))
                w1off_f = p2.enter_context(nc.sbuf_tensor("w1off_f",
                                                          [128, 8, NST], F32))
                w2off_f = p2.enter_context(nc.sbuf_tensor("w2off_f",
                                                          [128, 32, NST],
                                                          F32))
                imod8 = p2.enter_context(nc.sbuf_tensor("imod8", [8, 256],
                                                        F32))
                I8rep = p2.enter_context(nc.sbuf_tensor("I8rep", [8, 256],
                                                        F32))
                G_pm = p2.enter_context(nc.sbuf_tensor("G_pm", [E, D], F32))
                BT_pm = p2.enter_context(nc.sbuf_tensor("BT_pm", [E, D], F32))
                B1_pm = p2.enter_context(nc.sbuf_tensor("B1_pm", [E, FF],
                                                        F32))
                B2_pm = p2.enter_context(nc.sbuf_tensor("B2_pm", [E, D], F32))

                qG.dma(ge.dma_start(G_pm[:], gam_d))
                qG.dma(ge.dma_start(BT_pm[:], bet_d))
                qG.dma(ge.dma_start(B1_pm[:], b1r_d))
                qG.dma(ge.dma_start(B2_pm[:], b2r_d))
                m_gb = qG.n
                m_imod = cG.inc(ge.iota(
                    imod8[:].rearrange("p (a b) -> p a b", b=8),
                    [[0, 32], [1, 8]], base=0, channel_multiplier=0,
                    allow_small_or_imprecise_dtypes=True))
                cG.wait(ve, m_imod)
                m_i8rep = V(ve.tensor_scalar(I8rep[:], imod8[:],
                                             iota_p[0:8, 0:1], None,
                                             ALU.is_equal))

                with contextlib.ExitStack() as ps:
                    num_ps = [ps.enter_context(nc.psum_tensor(U("num_ps"),
                                                              [128, D], F32))
                              for i in range(2)]
                    sc_psF = ps.enter_context(nc.psum_tensor("sc_ps",
                                                             [128, 512], F32))
                    pos_psF = ps.enter_context(nc.psum_tensor("pos_ps",
                                                              [128, 512],
                                                              F32))
                    cnt_psF = ps.enter_context(nc.psum_tensor("cnt_ps",
                                                              [1, 512], F32))
                    baseB_psF = ps.enter_context(nc.psum_tensor("baseB_ps",
                                                                [128, 512],
                                                                F32))
                    sc_ps = sc_psF[:, 0:E]
                    pos_ps = pos_psF[:, 0:E]
                    cnt_ps = cnt_psF[:, 0:E]
                    baseB_ps = baseB_psF[:, 0:E]

                    m_x2_use = [0, 0]
                    m_x1_use_s = [0, 0]
                    m_x1_st = [0, 0]
                    m_fnT_use = [0, 0]
                    m_num_ev = [0, 0]
                    m_base_ev = 0
                    m_base_ev_is_g = True
                    m_sc_ev = 0
                    m_ofs_used = 0
                    m_pos_used = 0
                    m_pos_prev = 0

                    def load_x2(t, b):
                        cV.wait(sy, m_x2_use[b])
                        return xs2[b].dma(sy.dma_start(
                            x2_sb[b][:], x_d[t * 128:(t + 1) * 128, :]))

                    m_x2_ld = [0] * NT
                    m_x2_ld[0] = load_x2(0, 0)

                    for t in range(NT):
                        b = t % 2
                        seq = t // 16
                        if t + 1 < NT:
                            m_x2_ld[t + 1] = load_x2(t + 1, (t + 1) % 2)
                        if t == 0:
                            cV.wait(te, m_kv_ready[1])
                        cV.wait(te, m_kv_ready[seq])
                        cS.wait(te, m_num_ev[b])
                        for pc in range(2):
                            for h in range(2):
                                m_num = MM(num_ps[b][:, h * 512:(h + 1) * 512],
                                           qp_fm[:, pc, t * 128:(t + 1) * 128],
                                           kv_bf[:, seq, pc,
                                                 h * 512:(h + 1) * 512],
                                           start=(pc == 0), stop=(pc == 1),
                                           tag=(pc == 1 and h == 1))
                        cP.wait(se, m_num)
                        cV.wait(se, m_z_ready[t // 4])
                        m_ao = Sc(se.activation(ao_sb[:], num_ps[b][:],
                                                AF.Copy,
                                                scale=z_sb[:, t:t + 1]))
                        m_num_ev[b] = m_ao
                        cS.wait(ve, m_ao)
                        m_rstd2, m_nmr2 = stats(ao_sb)
                        cV.wait(se, m_nmr2)
                        m_t2 = Sc(se.activation(t2_sb[:], ao_sb[:],
                                                AF.Identity, bias=nmr_t[:],
                                                scale=rstd_t[:]))
                        xs2[b].wait(ve, m_x2_ld[t])
                        cS.wait(ve, m_t2)
                        qX1[b].wait(ve, m_x1_st[b])
                        cS.wait(ve, m_x1_use_s[b])
                        m_x1 = V(ve.tensor_tensor(x1_sb[b][:], x2_sb[b][:],
                                                  t2_sb[:], ALU.add))
                        m_x2_use[b] = m_x1
                        cV.wait(sy, m_x1)
                        m_x1_st[b] = qX1[b].dma(
                            sy.dma_start(x1_d[t * 128:(t + 1) * 128, :],
                                         x1_sb[b][:]))
                        m_rstd3, m_nmr3 = stats(x1_sb[b],
                                                rstd_out=s3_sb[:, t:t + 1],
                                                nmr_out=b3_sb[:, t:t + 1])
                        cV.wait(se, m_nmr3)
                        m_fe = Sc(se.activation(fe_sb[:], x1_sb[b][:],
                                                AF.Identity,
                                                bias=b3_sb[:, t:t + 1],
                                                scale=s3_sb[:, t:t + 1]))
                        m_x1_use_s[b] = m_fe
                        cS.wait(ve, m_fe)
                        m_rstdi, m_nmri = stats(fe_sb, rstd_out=rstd2f[:],
                                                nmr_out=nmr2f[:],
                                                mu_out=mu2f[:])
                        cS.wait(ve, m_rstdi)
                        m_ir = V(ve.reciprocal(ir2f[:], rstd2f[:]))
                        cV.wait(se, m_nmri)
                        m_fn = Sc(se.activation(fn_res[:, t, :], fe_sb[:],
                                                AF.Identity, bias=nmr2f[:],
                                                scale=rstd2f[:]))
                        cS.wait(sy, m_fn)
                        cP.wait(sy, m_fnT_use[b])
                        m_fnT = 0
                        for dc in range(8):
                            m_fnT = fT[b].dma(sy.dma_start(
                                fnT_sb[b][:, dc, :],
                                fn_res[:, t, dc * 128:(dc + 1) * 128],
                                transpose=True))
                        fT[b].wait(te, m_fnT)
                        cV.wait(te, m_sc_ev)
                        for dc in range(8):
                            m_sc_last = MM(sc_ps, fnT_sb[b][:, dc, :],
                                           ct_sb[:, dc, :],
                                           start=(dc == 0), stop=(dc == 7),
                                           tag=(dc == 7))
                        m_fnT_use[b] = m_sc_last
                        cP.wait(ve, m_sc_last)
                        cV.wait(ve, m_ir)
                        m_sc_ev = V(ve.tensor_scalar(sc_sb[:], sc_ps, ir2f[:],
                                                     None, ALU.mult))
                        cV.wait(ve, m_sc_ev)
                        m_sc2 = V(ve.scalar_tensor_tensor(sc2_sb[:], csumB[:],
                                                          mu2f[:], sc_sb[:],
                                                          ALU.mult, ALU.add))
                        cV.wait(ve, m_sc2)
                        m_mx = V(ve.max(mx8[:], sc2_sb[:]))
                        cV.wait(ve, m_mx)
                        m_mi = V(ve.max_index(mi8[:], mx8[:], sc2_sb[:]))
                        cV.wait(se, m_mx)
                        m_ea = Sc(se.activation(ea_sb[:], mx8[:, 0:1], AF.Exp,
                                                scale=-1.0))
                        cS.wait(ve, m_ea)
                        m_o1 = V(ve.tensor_scalar(on1[:], ea_sb[:], 1.0, None,
                                                  ALU.add))
                        cV.wait(ve, m_o1)
                        m_al = V(ve.reciprocal(alpha_sb[:, t:t + 1], on1[:]))
                        cV.wait(ve, m_mi)
                        cP.wait(ve, m_pos_prev)
                        m_mif = V(ve.tensor_copy(midx_f[:], mi8[:, 0:1]))
                        cV.wait(ve, m_mif)
                        m_oh = V(ve.tensor_scalar(oh_all[:, t, :], iota8[:],
                                                  midx_f[:], None,
                                                  ALU.is_equal))
                        cV.wait(ve, m_oh)
                        m_ohb = V(ve.tensor_copy(oh_bf[:], oh_all[:, t, :]))
                        if m_base_ev_is_g:
                            cG.wait(te, m_init_g)
                        else:
                            cV.wait(te, m_base_ev)
                        cV.wait(te, m_ofs_used)
                        m_bb = MM(baseB_ps, ones_f[:], base_row[:], start=True,
                                  stop=True, tag=True)
                        cP.wait(ve, m_bb)
                        m_ofs = V(ve.tensor_tensor(ofsB[:], baseB_ps,
                                                   mione[:], ALU.add))
                        m_ofs_used = m_ofs
                        cV.wait(te, m_ohb)
                        cV.wait(te, m_pos_used)
                        MM(pos_ps, tri_bf[:], oh_bf[:], start=True, stop=True,
                           tag=False)
                        m_pos = MM(cnt_ps, ones_bf[:, 0:1], oh_bf[:],
                                   start=True, stop=True, tag=True)
                        m_pos_prev = m_pos
                        cP.wait(ve, m_pos)
                        cV.wait(ve)
                        m_base_ev = V(ve.tensor_tensor(base_row[:],
                                                       base_row[:], cnt_ps,
                                                       ALU.add))
                        m_base_ev_is_g = False
                        cV.wait(ve, m_ofs)
                        m_t1 = V(ve.tensor_tensor(t1_sb[:], pos_ps, ofsB[:],
                                                  ALU.add))
                        m_pos_used = m_t1
                        cV.wait(ve, m_t1)
                        m_t1b = V(ve.tensor_tensor(t1_sb[:], t1_sb[:],
                                                   oh_all[:, t, :], ALU.mult))
                        cV.wait(ve, m_t1b)
                        V(ve.reduce_sum(pos_all[:, t:t + 1], t1_sb[:], AX.X))

                    m_fn_all = cS.n

                with contextlib.ExitStack() as ps:
                    ps_cnt = ps.enter_context(nc.psum_tensor("f_cnt", [8, 1],
                                                             F32))
                    ps_pre = ps.enter_context(nc.psum_tensor("f_pre", [8, 1],
                                                             F32))
                    ps_ste = ps.enter_context(nc.psum_tensor("f_ste",
                                                             [1, NST], F32))
                    ps_steB = ps.enter_context(nc.psum_tensor("f_steB",
                                                              [128, NST],
                                                              F32))
                    ps_brd = ps.enter_context(nc.psum_tensor("f_brd",
                                                             [1, 256], F32))
                    ps_bB = ps.enter_context(nc.psum_tensor("f_bB",
                                                            [128, 256], F32))
                    mRF_v, mRF_s = cV.n, cS.n
                    cV.wait(te, mRF_v)
                    cS.wait(te, mRF_s)
                    cG.wait(te)
                    cV.wait(ve, mRF_v)
                    cS.wait(ve, mRF_s)
                    m1 = MM(ps_cnt[:], base_row[:], ones_f[0:1, 0:1],
                            start=True, stop=True, tag=True)
                    cP.wait(ve, m1)
                    e1 = V(ve.tensor_copy(n_col_sb[:], ps_cnt[:]))
                    cV.wait(ve, e1)
                    e2 = V(ve.tensor_scalar(cmp8[:], thrT[:, 0:8],
                                            n_col_sb[:], None, ALU.is_lt))
                    cV.wait(ve, e2)
                    e3 = V(ve.reduce_sum(r8[:], cmp8[:], AX.X))
                    cV.wait(ve, e3)
                    e4 = V(ve.tensor_scalar(c_al_sb[:], r8[:], float(A), None,
                                            ALU.mult))
                    cV.wait(te, e4)
                    m2 = MM(ps_pre[:], sU8[:], c_al_sb[:], start=True,
                            stop=True, tag=True)
                    cP.wait(ve, m2)
                    e5 = V(ve.tensor_copy(B_col_sb[:], ps_pre[:]))
                    cV.wait(ve, e5)
                    e6 = V(ve.tensor_scalar(cmp2_sb[:], thrT[:, 0:NST],
                                            B_col_sb[:], None, ALU.is_ge))
                    cV.wait(te, e6)
                    m3 = MM(ps_ste[:], ones8[:], cmp2_sb[:], start=True,
                            stop=True, tag=True)
                    cP.wait(ve, m3)
                    e7 = V(ve.tensor_scalar(st_e_sb[:], ps_ste[:], -1.0,
                                            None, ALU.add))
                    cV.wait(te, e7)
                    m4 = MM(ps_steB[:], ones_f[:], st_e_sb[:],
                            start=True, stop=True, tag=True)
                    cP.wait(ve, m4)
                    e8 = V(ve.tensor_copy(st_eB_sb[:], ps_steB[:]))
                    m5 = MM(ps_brd[:], B_col_sb[:], I8rep[:], start=True,
                            stop=True, tag=True)
                    cP.wait(ve, m5)
                    e9 = V(ve.tensor_copy(Brow32_sb[:], ps_brd[:]))
                    cV.wait(te, e9)
                    m6 = MM(ps_bB[:], ones_f[:], Brow32_sb[:],
                            start=True, stop=True, tag=True)
                    cP.wait(ve, m6)
                    e10 = V(ve.tensor_tensor(
                        prod_sb[:].rearrange("p t e -> p (t e)"),
                        oh_all[:].rearrange("p t e -> p (t e)"),
                        ps_bB[:], ALU.mult))
                    cV.wait(ve, e10)
                    e11 = V(ve.reduce_sum(red_sb[:], prod_sb[:], AX.X))
                    cV.wait(ve, e11)
                    e12 = V(ve.tensor_tensor(destf_sb[:], red_sb[:],
                                             pos_all[:], ALU.add))
                    cV.wait(ve, e12)
                    m_dest_all = V(ve.tensor_copy(dest_sb[:], destf_sb[:]))
                    cV.wait(ve, e8)
                    eo = 0
                    for dc in range(8):
                        eo = V(ve.tensor_scalar(w1off_f[:, dc, :],
                                                st_eB_sb[:], float(D),
                                                iota_pdc[:, dc:dc + 1],
                                                ALU.mult, ALU.add))
                    for fc in range(32):
                        eo = V(ve.tensor_scalar(w2off_f[:, fc, :],
                                                st_eB_sb[:], float(FF),
                                                iota_pfc[:, fc:fc + 1],
                                                ALU.mult, ALU.add))
                    cV.wait(ve, eo)
                    V(ve.tensor_copy(
                        w1off_u[:].rearrange("p a b -> p (a b)"),
                        w1off_f[:].rearrange("p a b -> p (a b)")))
                    m_offs_ready = V(ve.tensor_copy(
                        w2off_u[:].rearrange("p a b -> p (a b)"),
                        w2off_f[:].rearrange("p a b -> p (a b)")))
                    m_f1_done = cV.n
                with contextlib.ExitStack() as ps:
                    ps_ohm = ps.enter_context(nc.psum_tensor("f_ohm",
                                                             [8, NST], F32))
                    ps_b2 = [ps.enter_context(nc.psum_tensor(f"f_b2{h}",
                                                             [NST, 512], F32))
                             for h in range(2)]
                    ps_sel = [ps.enter_context(nc.psum_tensor(f"f_sel{i}",
                                                              [128, NST],
                                                              F32))
                              for i in range(2)]
                    cV.wait(te, m_f1_done)
                    m7 = MM(ps_ohm[:], D8[:], cmp2_sb[:], start=True,
                            stop=True, tag=True)
                    cP.wait(ve, m7)
                    e14 = V(ve.tensor_copy(ohmat_sb[:], ps_ohm[:]))
                    qG.wait(te, m_gb)
                    cV.wait(te, e14)
                    for h in range(2):
                        m8 = MM(ps_b2[h][:], ohmat_sb[:],
                                B2_pm[:, h * 512:(h + 1) * 512], start=True,
                                stop=True, tag=(h == 1))
                    cP.wait(ve, m8)
                    V(ve.tensor_copy(b2all_sb[:, 0:512], ps_b2[0][:]))
                    e15 = V(ve.tensor_copy(b2all_sb[:, 512:1024],
                                           ps_b2[1][:]))
                    cV.wait(sy, e15)
                    qB.dma(sy.dma_start(b2scr_2d, b2all_sb[:]))
                    m_sel_ev = [0, 0]
                    k = 0
                    for pm, dall, nch in ((G_pm, g_all, 8), (BT_pm, bt_all, 8),
                                          (B1_pm, b1_all, 32)):
                        for c in range(nch):
                            r = k % 2
                            cV.wait(te, m_sel_ev[r])
                            msel = MM(ps_sel[r][:],
                                      pm[:, c * 128:(c + 1) * 128],
                                      ohmat_sb[:], start=True, stop=True,
                                      tag=True)
                            cP.wait(ve, msel)
                            m_sel_ev[r] = V(ve.tensor_copy(dall[:, c, :],
                                                           ps_sel[r][:]))
                            k += 1
                    m_gall = cV.n

                cV.wait(ge, m_dest_all)
                cS.wait(ge, m_fn_all)
                qZ.wait(ge)
                for t in range(NT):
                    qIs[t % 2].dma(ge.indirect_dma_start(
                        fn_tbl,
                        bass.IndirectOffsetOnAxis(ap=dest_sb[:, t:t + 1],
                                                  axis=0),
                        fn_res[:, t, :], None, bounds_check=None))

        mC_v, mC_s, mC_p = cV.n, cS.n, cP.n
        for eng in (te, ve, se, sy, ge):
            cV.wait(eng, mC_v)
            cS.wait(eng, mC_s)
            cP.wait(eng, mC_p)
            for qq in (qIs[0], qIs[1], qX1[0], qX1[1], xs2[0], xs2[1],
                       fT[0], fT[1], qB):
                qq.wait(eng)

        ff = contextlib.ExitStack()
        with ff:
            w1_sb = ff.enter_context(nc.sbuf_tensor("w1_sb", [128, 8, FF],
                                                    BF16))
            w2_sb = ff.enter_context(nc.sbuf_tensor("w2_sb", [128, 32, D],
                                                    BF16))
            hr_sb = ff.enter_context(nc.sbuf_tensor("hr_sb", [128, 8, 512],
                                                    BF16))
            h_sb = ff.enter_context(nc.sbuf_tensor("h_sb", [128, 8, 512],
                                                   BF16))
            a1r = ff.enter_context(nc.sbuf_tensor("a1r", [128, 32, 512],
                                                  BF16))
            y2_sb = ff.enter_context(nc.sbuf_tensor("y2_sb", [128, D], F32))
            b2B = ff.enter_context(nc.sbuf_tensor("b2B", [128, D], F32))
            b2row = [ff.enter_context(nc.sbuf_tensor(f"b2row{i}", [1, D],
                                                     F32))
                     for i in range(2)]

            with contextlib.ExitStack() as ps:
                a1_ps = [ps.enter_context(nc.psum_tensor(U("a1_ps"),
                                                         [128, 512], F32))
                         for i in range(2)]
                y2_ps = [ps.enter_context(nc.psum_tensor(U("y2_ps"), [128, D],
                                                         F32))
                         for i in range(2)]
                b2B_ps = ps.enter_context(nc.psum_tensor("b2B_ps", [128, D],
                                                         F32))

                m_w1_use = 0
                m_w2_use = 0
                m_h_use = 0
                m_hr_free = 0
                m_a1_ev_s = [0, 0]
                m_a1_ev_v = [0, 0]
                m_y2_ev = [0, 0]
                m_y2_st = 0
                m_b2B_use = 0
                m_b2B = 0
                m_b2mm_l = [0, 0]
                m_b2ld = [0, 0]

                for s in range(NST):
                    cP.wait(ge, m_w1_use)
                    if s == 0:
                        cV.wait(ge, m_offs_ready)
                    m_w1_ld = 0
                    for dc in range(8):
                        m_w1_ld = qW1.dma(ge.indirect_dma_start(
                            w1_sb[:, dc, :], None, w1_lin,
                            bass.IndirectOffsetOnAxis(
                                ap=w1off_u[:, dc, s:s + 1], axis=0),
                            bounds_check=None))
                    cP.wait(ge, m_w2_use)
                    m_w2_ld = 0
                    for fc in range(32):
                        m_w2_ld = qW2.dma(ge.indirect_dma_start(
                            w2_sb[:, fc, :], None, w2_lin,
                            bass.IndirectOffsetOnAxis(
                                ap=w2off_u[:, fc, s:s + 1], axis=0),
                            bounds_check=None))
                    bb = s % 2
                    cP.wait(sy, m_b2mm_l[bb])
                    if s == 0:
                        qB.wait(sy)
                    m_b2ld[bb] = qB.dma(sy.dma_start(b2row[bb][:],
                                                     b2scr_2d[s:s + 1, :]))
                    cV.wait(te, m_b2B_use)
                    qB.wait(te, m_b2ld[bb])
                    for h in range(2):
                        m_b2mm = MM(b2B_ps[:, h * 512:(h + 1) * 512],
                                    ones_f[:],
                                    b2row[bb][:, h * 512:(h + 1) * 512],
                                    start=True, stop=True, tag=(h == 1))
                    m_b2mm_l[bb] = m_b2mm
                    cP.wait(ve, m_b2mm)
                    m_b2B = V(ve.tensor_copy(b2B[:], b2B_ps[:]))
                    cP.wait(se, m_h_use)
                    if s == 0:
                        cV.wait(se, m_gall)
                    cS.wait(sy, m_hr_free)
                    m_htr = 0
                    for dc in range(8):
                        m_htr = hT.dma(sy.dma_start(
                            hr_sb[:, dc, :],
                            fn_tbl[s * 512:(s + 1) * 512,
                                   dc * 128:(dc + 1) * 128],
                            transpose=True))
                    hT.wait(se, m_htr)
                    m_h = 0
                    for dc in range(8):
                        m_h = Sc(se.activation(h_sb[:, dc, :],
                                               hr_sb[:, dc, :],
                                               AF.Identity,
                                               bias=bt_all[:, dc, s:s + 1],
                                               scale=g_all[:, dc, s:s + 1]))
                    m_hr_free = m_h
                    qW1.wait(te, m_w1_ld)
                    cS.wait(te, m_h)
                    for fc in range(32):
                        ar = fc % 2
                        cS.wait(te, m_a1_ev_s[ar])
                        cV.wait(te, m_a1_ev_v[ar])
                        for dc in range(8):
                            m_a1mm = MM(a1_ps[ar][:],
                                        w1_sb[:, dc, fc * 128:(fc + 1) * 128],
                                        h_sb[:, dc, :],
                                        start=(dc == 0), stop=(dc == 7),
                                        tag=(dc == 7))
                        if fc % 2 == 0:
                            cP.wait(se, m_a1mm)
                            m_a1_ev_s[ar] = Sc(se.activation(
                                a1r[:, fc, :], a1_ps[ar][:], AF.Relu,
                                bias=b1_all[:, fc, s:s + 1]))
                        else:
                            cP.wait(ve, m_a1mm)
                            m_a1_ev_v[ar] = V(ve.tensor_scalar(
                                a1r[:, fc, :], a1_ps[ar][:],
                                b1_all[:, fc, s:s + 1], 0.0,
                                ALU.add, ALU.max))
                    m_w1_use = cP.n
                    m_h_use = cP.n
                    qW2.wait(te, m_w2_ld)
                    m_s1s = cS.n
                    m_s1v = cV.n
                    cS.wait(te, m_s1s)
                    cV.wait(te, m_s1v)
                    for sub in range(4):
                        yr = sub % 2
                        cV.wait(te, m_y2_ev[yr])
                        for fc in range(32):
                            for h in range(2):
                                m_y2mm = MM(
                                    y2_ps[yr][:, h * 512:(h + 1) * 512],
                                    a1r[:, fc, sub * 128:(sub + 1) * 128],
                                    w2_sb[:, fc, h * 512:(h + 1) * 512],
                                    start=(fc == 0), stop=(fc == 31),
                                    tag=(fc == 31 and h == 1))
                        cP.wait(ve, m_y2mm)
                        cV.wait(ve, m_b2B)
                        qY2.wait(ve, m_y2_st)
                        m_y2_ev[yr] = V(ve.scalar_tensor_tensor(
                            y2_sb[:], y2_ps[yr][:], 0.0, b2B[:],
                            ALU.add, ALU.add))
                        cV.wait(sy, m_y2_ev[yr])
                        m_y2_st = qY2.dma(sy.dma_start(
                            y2_tbl[s * 512 + sub * 128:
                                   s * 512 + (sub + 1) * 128, :],
                            y2_sb[:]))
                    m_w2_use = cP.n
                    m_b2B_use = cV.n

        mD_v, mD_s, mD_p = cV.n, cS.n, cP.n
        for eng in (ge, sy, se, ve):
            cV.wait(eng, mD_v)
            cS.wait(eng, mD_s)
            cP.wait(eng, mD_p)
            for qq in (qY2, qW1, qW2, hT, qB):
                qq.wait(eng)

        cb = contextlib.ExitStack()
        with cb:
            y2g = [cb.enter_context(nc.sbuf_tensor(f"y2g{i}", [128, D], F32))
                   for i in range(2)]
            x1c = [cb.enter_context(nc.sbuf_tensor(f"x1c{i}", [128, D], F32))
                   for i in range(2)]
            fe2 = cb.enter_context(nc.sbuf_tensor("fe2", [128, D], F32))
            of_sb = cb.enter_context(nc.sbuf_tensor("of_sb", [128, D], F32))
            t5_sb = cb.enter_context(nc.sbuf_tensor("t5_sb", [128, D], F32))
            xo_sb = [cb.enter_context(nc.sbuf_tensor(f"xo_sb{i}", [128, D],
                                                     F32))
                     for i in range(2)]
            bn6b = cb.enter_context(nc.sbuf_tensor("bn6b", [128, 12], F32))
            mv2b = cb.enter_context(nc.sbuf_tensor("mv2b", [128, 2], F32))
            varb = cb.enter_context(nc.sbuf_tensor("varb", [128, 1], F32))
            lnvb = cb.enter_context(nc.sbuf_tensor("lnvb", [128, 1], F32))
            rstdb = cb.enter_context(nc.sbuf_tensor("rstdb", [128, 1], F32))
            nmrb = cb.enter_context(nc.sbuf_tensor("nmrb", [128, 1], F32))

            st5 = {"se": 0, "ve": 0}

            def stats5(src):
                cS.wait(ve, st5["se"])
                cV.wait(ve)
                V(ve.bn_stats(bn6b[:, 0:6], src[:, 0:512]))
                m1 = V(ve.bn_stats(bn6b[:, 6:12], src[:, 512:1024]))
                cV.wait(ve, m1)
                m2 = V(ve.bn_aggr(mv2b[:], bn6b[:]))
                cV.wait(ve, m2)
                m3 = V(ve.tensor_scalar(varb[:], mv2b[:, 1:2], EPS_LN, None,
                                        ALU.add))
                cV.wait(se, m3)
                m4 = Sc(se.activation(lnvb[:], varb[:], AF.Ln))
                cS.wait(se, m4)
                m5 = Sc(se.activation(rstdb[:], lnvb[:], AF.Exp, scale=-0.5))
                st5["se"] = m5
                cS.wait(ve, m5)
                m6 = V(ve.tensor_scalar(nmrb[:], mv2b[:, 0:1], rstdb[:], -1.0,
                                        ALU.mult, ALU.mult))
                st5["ve"] = m6
                return m5, m6

            m_y2g_use = [0, 0]
            m_x1c_use_s = [0, 0]
            m_x1c_use_v = [0, 0]
            m_xo_st = [0, 0]

            def load_cb(t, b):
                cV.wait(ge, m_y2g_use[b])
                mg = qIg[b].dma(ge.indirect_dma_start(
                    y2g[b][:], None,
                    y2_tbl,
                    bass.IndirectOffsetOnAxis(ap=dest_sb[:, t:t + 1], axis=0),
                    bounds_check=None))
                cS.wait(sy, m_x1c_use_s[b])
                cV.wait(sy, m_x1c_use_v[b])
                mx = qXC[b].dma(sy.dma_start(x1c[b][:],
                                             x1_d[t * 128:(t + 1) * 128, :]))
                return mg, mx

            m_g = [None] * NT
            m_g[0] = load_cb(0, 0)

            for t in range(NT):
                b = t % 2
                if t + 1 < NT:
                    m_g[t + 1] = load_cb(t + 1, (t + 1) % 2)
                mg, mx = m_g[t]
                qXC[b].wait(se, mx)
                m_fe2 = Sc(se.activation(fe2[:], x1c[b][:], AF.Identity,
                                         bias=b3_sb[:, t:t + 1],
                                         scale=s3_sb[:, t:t + 1]))
                qIg[b].wait(ve, mg)
                cS.wait(ve, m_fe2)
                m_of = V(ve.scalar_tensor_tensor(
                    of_sb[:], y2g[b][:], alpha_sb[:, t:t + 1], fe2[:],
                    ALU.mult, ALU.add))
                m_y2g_use[b] = m_of
                cV.wait(ve, m_of)
                m_r5, m_n5 = stats5(of_sb)
                cV.wait(se, m_n5)
                m_t5 = Sc(se.activation(t5_sb[:], of_sb[:], AF.Identity,
                                        bias=nmrb[:], scale=rstdb[:]))
                m_x1c_use_s[b] = m_t5
                cS.wait(ve, m_t5)
                qO[b].wait(ve, m_xo_st[b])
                m_x2o = V(ve.tensor_tensor(xo_sb[b][:], x1c[b][:], t5_sb[:],
                                           ALU.add))
                m_x1c_use_v[b] = m_x2o
                cV.wait(sy, m_x2o)
                m_xo_st[b] = qO[b].dma(sy.dma_start(
                    out_d[t * 128:(t + 1) * 128, :], xo_sb[b][:]))

        qO[0].wait(sy)
        qO[1].wait(sy)
        qIg[0].wait(ge)
        qIg[1].wait(ge)
    return nc


_NC_CACHE = {}


def _get_nc():
    key = 0
    if key not in _NC_CACHE:
        _NC_CACHE[key] = build()
    return _NC_CACHE[key]


def prep_in_maps(inputs):
    x = np.ascontiguousarray(np.asarray(inputs["L0_attn_x"], dtype=np.float32))
    wq = np.asarray(inputs["L0_attn_Wq"], dtype=np.float32)
    wk = np.asarray(inputs["L0_attn_Wk"], dtype=np.float32)
    wv = np.asarray(inputs["L0_attn_Wv"], dtype=np.float32)
    proj = np.ascontiguousarray(np.asarray(inputs["L0_attn_proj"],
                                           dtype=np.float32))
    cent = np.asarray(inputs["L0_ffn_centroids"], dtype=np.float32)
    gam = np.ascontiguousarray(np.asarray(inputs["L0_ffn_gamma"],
                                          dtype=np.float32))
    bet = np.ascontiguousarray(np.asarray(inputs["L0_ffn_beta"],
                                          dtype=np.float32))
    W1 = np.asarray(inputs["L0_ffn_W1"], dtype=np.float32)
    b1 = np.ascontiguousarray(np.asarray(inputs["L0_ffn_b1"],
                                         dtype=np.float32))
    W2 = np.asarray(inputs["L0_ffn_W2"], dtype=np.float32)
    b2 = np.ascontiguousarray(np.asarray(inputs["L0_ffn_b2"],
                                         dtype=np.float32))

    wqT = np.ascontiguousarray(wq.T)
    wkT = np.ascontiguousarray(wk.T)
    wvT = np.ascontiguousarray(wv.T)
    cT = np.ascontiguousarray(cent.T)
    w1T = np.ascontiguousarray(W1.transpose(0, 2, 1))
    w2T = np.ascontiguousarray(W2.transpose(0, 2, 1))

    in_maps = []
    for c in range(NCORE):
        xm = np.ascontiguousarray(x[2 * c:2 * c + 2].reshape(T, D))
        in_maps.append({
            "x": xm, "wqT": wqT, "wkT": wkT, "wvT": wvT, "proj": proj,
            "cT": cT, "gam": gam, "bet": bet, "b1r": b1, "b2r": b2,
            "w1T": w1T, "w2T": w2T,
        })
    return in_maps


def kernel(**inputs):
    in_maps = prep_in_maps(inputs)
    nc = _get_nc()
    res = run_bass_kernel_spmd(nc, in_maps, core_ids=list(range(NCORE)))
    outs = [res.results[c]["out"].reshape(2, 2048, D) for c in range(NCORE)]
    full = np.concatenate(outs, axis=0)
    return full


# version 08
# speedup vs baseline: 1.0057x; 1.0057x over previous
import contextlib
import numpy as np
import concourse.bass as bass
import concourse.mybir as mybir
from concourse.bass_utils import run_bass_kernel_spmd

F32 = mybir.dt.float32
BF16 = mybir.dt.bfloat16
U32 = mybir.dt.uint32
AF = mybir.ActivationFunctionType
ALU = mybir.AluOpType
AX = mybir.AxisListType

NCORE = 8
D = 1024
PD = 256
FF = 4096
E = 8
T = 4096
NBT = 8
NT = 32
A = 512
NST = 15
SLOTS = NST * A
EPS_LN = 1e-5
EPS_Z = 1e-8


class S:
    def __init__(self, nc, es, name):
        self.h = es.enter_context(nc.semaphore(name))
        self.n = 0

    def dma(self, instr):
        instr.then_inc(self.h, 16)
        self.n += 16
        return self.n

    def inc(self, instr):
        instr.then_inc(self.h, 1)
        self.n += 1
        return self.n

    def wait(self, eng, n=None):
        n = self.n if n is None else n
        if n > 0:
            eng.wait_ge(self.h, n)


def build():
    nc = bass.Bass("TRN2", target_bir_lowering=False)
    _uid = [0]

    def U(name):
        _uid[0] += 1
        return f"{name}_{_uid[0]}"

    te, ve, se, ge, sy = nc.tensor, nc.vector, nc.scalar, nc.gpsimd, nc.sync

    x_d = nc.dram_tensor("x", [T, D], F32, kind="ExternalInput").ap()
    wqT_d = nc.dram_tensor("wqT", [D, D], F32, kind="ExternalInput").ap()
    wkT_d = nc.dram_tensor("wkT", [D, D], F32, kind="ExternalInput").ap()
    wvT_d = nc.dram_tensor("wvT", [D, D], F32, kind="ExternalInput").ap()
    proj_d = nc.dram_tensor("proj", [D, PD], F32, kind="ExternalInput").ap()
    cT_d = nc.dram_tensor("cT", [D, E], F32, kind="ExternalInput").ap()
    gam_d = nc.dram_tensor("gam", [E, D], F32, kind="ExternalInput").ap()
    bet_d = nc.dram_tensor("bet", [E, D], F32, kind="ExternalInput").ap()
    b1r_d = nc.dram_tensor("b1r", [E, FF], F32, kind="ExternalInput").ap()
    b2r_d = nc.dram_tensor("b2r", [E, D], F32, kind="ExternalInput").ap()
    w1T_d = nc.dram_tensor("w1T", [E, D, FF], F32, kind="ExternalInput").ap()
    w2T_d = nc.dram_tensor("w2T", [E, FF, D], F32, kind="ExternalInput").ap()

    out_d = nc.dram_tensor("out", [T, D], F32, kind="ExternalOutput").ap()
    x1_d = nc.dram_tensor("x1d", [T, D], F32, kind="Internal").ap()
    fn_tbl = nc.dram_tensor("fn_tbl", [SLOTS, D], BF16, kind="Internal").ap()
    y2_tbl = nc.dram_tensor("y2_tbl", [SLOTS, D], F32, kind="Internal").ap()
    b2scr_d = nc.dram_tensor("b2scr", [1, NST * D], F32, kind="Internal").ap()
    b2scr_2d = b2scr_d.rearrange("o (s d) -> (o s) d", d=D)

    w1_lin = w1T_d.rearrange("e d f -> (e d) f")
    w2_lin = w2T_d.rearrange("e f d -> (e f) d")

    es = contextlib.ExitStack()
    with es:
        xs1 = [S(nc, es, f"xs1_{i}") for i in range(2)]
        xs2 = [S(nc, es, f"xs2_{i}") for i in range(2)]
        uT = [S(nc, es, f"uT_{i}") for i in range(2)]
        fT = [S(nc, es, f"fT_{i}") for i in range(2)]
        hT = S(nc, es, "hT")
        qWI = S(nc, es, "qWI")
        qW1 = S(nc, es, "qW1")
        qW2 = S(nc, es, "qW2")
        qIs = [S(nc, es, f"qIs_{i}") for i in range(2)]
        qIg = [S(nc, es, f"qIg_{i}") for i in range(2)]
        qX1 = [S(nc, es, f"qX1_{i}") for i in range(2)]
        qXC = [S(nc, es, f"qXC_{i}") for i in range(2)]
        qY2 = S(nc, es, "qY2")
        qO = [S(nc, es, f"qO_{i}") for i in range(2)]
        qB = S(nc, es, "qB")
        qZ = S(nc, es, "qZ")
        qG = S(nc, es, "qG")
        cP = S(nc, es, "cP")
        cV = S(nc, es, "cV")
        cS = S(nc, es, "cS")
        cG = S(nc, es, "cG")

        def V(instr):
            return cV.inc(instr)

        def Sc(instr):
            return cS.inc(instr)

        def MM(out, lhsT, rhs, start, stop, tag=False):
            ins = te.matmul(out, lhsT, rhs, start=start, stop=stop)
            if tag:
                return cP.inc(ins)
            return None

        ones_bf = es.enter_context(nc.sbuf_tensor("ones_bf", [128, 128], BF16))
        ones_f = es.enter_context(nc.sbuf_tensor("ones_f", [1, 128], F32))
        tri_bf = es.enter_context(nc.sbuf_tensor("tri_bf", [128, 128], BF16))
        iota8 = es.enter_context(nc.sbuf_tensor("iota8", [128, 8], F32))
        mione = es.enter_context(nc.sbuf_tensor("mione", [128, 8], F32))
        base_row = es.enter_context(nc.sbuf_tensor("base_row", [1, 8], F32))
        ct_sb = es.enter_context(nc.sbuf_tensor("ct_sb", [128, 8, E], BF16))
        csum_row = es.enter_context(nc.sbuf_tensor("csum_row", [1, E], F32))
        csumB = es.enter_context(nc.sbuf_tensor("csumB", [128, E], F32))
        alpha_sb = es.enter_context(nc.sbuf_tensor("alpha_sb", [128, NT], F32))
        dest_sb = es.enter_context(nc.sbuf_tensor("dest_sb", [128, NT], U32))
        s3_sb = es.enter_context(nc.sbuf_tensor("s3_sb", [128, NT], F32))
        b3_sb = es.enter_context(nc.sbuf_tensor("b3_sb", [128, NT], F32))
        z_sb = es.enter_context(nc.sbuf_tensor("z_sb", [128, NT], F32))
        pos_all = es.enter_context(nc.sbuf_tensor("pos_all", [128, NT], F32))
        oh_all = es.enter_context(nc.sbuf_tensor("oh_all", [128, NT, E], F32))
        iota_p = es.enter_context(nc.sbuf_tensor("iota_p", [128, 1], F32))
        iota_pm1 = es.enter_context(nc.sbuf_tensor("iota_pm1", [128, 1], F32))
        iota_pdc = es.enter_context(nc.sbuf_tensor("iota_pdc", [128, 8], F32))
        iota_pfc = es.enter_context(nc.sbuf_tensor("iota_pfc", [128, 32], F32))
        thrT = es.enter_context(nc.sbuf_tensor("thrT", [8, 16], F32))
        I8 = es.enter_context(nc.sbuf_tensor("I8", [8, 8], F32))
        subD8 = es.enter_context(nc.sbuf_tensor("subD8", [8, 8], F32))
        D8 = es.enter_context(nc.sbuf_tensor("D8", [8, 8], F32))
        sU8 = es.enter_context(nc.sbuf_tensor("sU8", [8, 8], F32))
        ones8 = es.enter_context(nc.sbuf_tensor("ones8", [8, 1], F32))
        g_all = es.enter_context(nc.sbuf_tensor("g_all", [128, 8, NST], F32))
        bt_all = es.enter_context(nc.sbuf_tensor("bt_all", [128, 8, NST], F32))
        b1_all = es.enter_context(nc.sbuf_tensor("b1_all", [128, 32, NST], F32))
        w1off_u = es.enter_context(nc.sbuf_tensor("w1off_u", [128, 8, NST],
                                                  U32))
        w2off_u = es.enter_context(nc.sbuf_tensor("w2off_u", [128, 32, NST],
                                                  U32))

        blk = es.enter_context(nc.Block())

        cG.inc(ge.memset(ones_bf[:], 1.0))
        cG.inc(ge.memset(ones_f[:], 1.0))
        cG.wait(ge)
        cG.inc(ge.affine_select(tri_bf[:], ones_bf[:], [[1, 128]], ALU.is_ge,
                                0.0, base=0, channel_multiplier=-1))
        cG.inc(ge.iota(iota8[:], [[1, 8]], base=0, channel_multiplier=0,
                       allow_small_or_imprecise_dtypes=True))
        cG.inc(ge.memset(mione[:], -1.0))
        cG.inc(ge.memset(ones8[:], 1.0))
        cG.inc(ge.iota(iota_p[:], [[0, 1]], base=0, channel_multiplier=1,
                       allow_small_or_imprecise_dtypes=True))
        cG.inc(ge.iota(iota_pm1[:], [[0, 1]], base=-1, channel_multiplier=1,
                       allow_small_or_imprecise_dtypes=True))
        cG.inc(ge.iota(iota_pdc[:], [[128, 8]], base=0, channel_multiplier=1,
                       allow_small_or_imprecise_dtypes=True))
        cG.inc(ge.iota(iota_pfc[:], [[128, 32]], base=0, channel_multiplier=1,
                       allow_small_or_imprecise_dtypes=True))
        cG.inc(ge.iota(thrT[:], [[A, 16]], base=0, channel_multiplier=0,
                       allow_small_or_imprecise_dtypes=True))
        m_init_g = cG.inc(ge.memset(base_row[:], 0.0))
        cG.wait(ve, m_init_g)
        V(ve.tensor_scalar(I8[:], iota8[0:8, :], iota_p[0:8, 0:1], None,
                           ALU.is_equal))
        V(ve.tensor_scalar(subD8[:], iota8[0:8, :], iota_pm1[0:8, 0:1], None,
                           ALU.is_equal))
        V(ve.tensor_scalar(sU8[:], iota8[0:8, :], iota_p[0:8, 0:1], None,
                           ALU.is_gt))
        cV.wait(ve)
        m_init_v = V(ve.tensor_tensor(D8[:], I8[:], subD8[:], ALU.subtract))
        qWI.dma(ge.dma_start(ct_sb[:], cT_d.rearrange("(c p) e -> p c e",
                                                      p=128)))

        ab = contextlib.ExitStack()
        with ab:
            qp_fm = ab.enter_context(nc.sbuf_tensor("qp_fm", [128, 2, T],
                                                    BF16))
            kv_bf = ab.enter_context(nc.sbuf_tensor("kv_bf", [128, 2, 2, D],
                                                    BF16))
            bn6 = ab.enter_context(nc.sbuf_tensor("bn6", [128, 12], F32))
            mv2 = ab.enter_context(nc.sbuf_tensor("mv2", [128, 2], F32))
            varp = ab.enter_context(nc.sbuf_tensor("varp", [128, 1], F32))
            lnv = ab.enter_context(nc.sbuf_tensor("lnv", [128, 1], F32))
            rstd_t = ab.enter_context(nc.sbuf_tensor("rstd_t", [128, 1], F32))
            nmr_t = ab.enter_context(nc.sbuf_tensor("nmr_t", [128, 1], F32))

            stx = {"se": 0, "ve": 0}

            def stats(src, rstd_out=None, nmr_out=None, mu_out=None):
                rstd_out = rstd_t[:] if rstd_out is None else rstd_out
                nmr_out = nmr_t[:] if nmr_out is None else nmr_out
                cS.wait(ve, stx["se"])
                cV.wait(ve)
                V(ve.bn_stats(bn6[:, 0:6], src[:, 0:512]))
                m1 = V(ve.bn_stats(bn6[:, 6:12], src[:, 512:1024]))
                cV.wait(ve, m1)
                m2 = V(ve.bn_aggr(mv2[:], bn6[:]))
                cV.wait(ve, m2)
                m3 = V(ve.tensor_scalar(varp[:], mv2[:, 1:2], EPS_LN, None,
                                        ALU.add))
                if mu_out is not None:
                    V(ve.tensor_copy(mu_out, mv2[:, 0:1]))
                cV.wait(se, m3)
                m4 = Sc(se.activation(lnv[:], varp[:], AF.Ln))
                cS.wait(se, m4)
                m5 = Sc(se.activation(rstd_out, lnv[:], AF.Exp, scale=-0.5))
                stx["se"] = m5
                cS.wait(ve, m5)
                m6 = V(ve.tensor_scalar(nmr_out, mv2[:, 0:1], rstd_out, -1.0,
                                        ALU.mult, ALU.mult))
                stx["ve"] = m6
                return m5, m6

            at = contextlib.ExitStack()
            with at:
                wq_sb = at.enter_context(nc.sbuf_tensor("wq_sb", [128, 8, D],
                                                        BF16))
                wk_sb = at.enter_context(nc.sbuf_tensor("wk_sb", [128, 8, D],
                                                        BF16))
                wv_sb = at.enter_context(nc.sbuf_tensor("wv_sb", [128, 8, D],
                                                        BF16))
                projn = at.enter_context(nc.sbuf_tensor("projn", [128, 8, PD],
                                                        BF16))
                x_sb = [at.enter_context(nc.sbuf_tensor(f"x_sb{i}", [128, D],
                                                        F32))
                        for i in range(2)]
                u_tm = at.enter_context(nc.sbuf_tensor("u_tm", [128, 2, D],
                                                       BF16))
                u_fm = [at.enter_context(nc.sbuf_tensor(f"u_fm{i}",
                                                        [128, 8, 512], BF16))
                        for i in range(2)]
                qev = at.enter_context(nc.sbuf_tensor("qev", [128, 2, 512],
                                                      BF16))
                kev = at.enter_context(nc.sbuf_tensor("kev", [128, 8, 512],
                                                      BF16))
                kp_tm = at.enter_context(nc.sbuf_tensor("kp_tm", [128, 4, PD],
                                                        BF16))
                v_tm = [at.enter_context(nc.sbuf_tensor(f"v_tm{i}",
                                                        [128, 4, D], BF16))
                        for i in range(2)]
                zt_sb = at.enter_context(nc.sbuf_tensor("zt_sb",
                                                        [128, 4, 1024], BF16))
                kv_sb = at.enter_context(nc.sbuf_tensor("kv_sb", [128, 2, D],
                                                        F32))
                el_t = at.enter_context(nc.sbuf_tensor("el_t", [128, 1024],
                                                       F32))
                el_e = at.enter_context(nc.sbuf_tensor("el_e", [128, 1024],
                                                       F32))
                el_r = at.enter_context(nc.sbuf_tensor("el_r", [128, 1024],
                                                       F32))

                m_zt = cG.inc(ge.memset(zt_sb[:], 0.0))
                cG.wait(sy, m_zt)
                for i in range(SLOTS // 512):
                    qZ.dma(sy.dma_start(
                        fn_tbl[i * 512:(i + 1) * 512, :].rearrange(
                            "(p a) f -> p a f", p=128),
                        zt_sb[:]))
                qWI.dma(ge.dma_start(wq_sb[:],
                                     wqT_d.rearrange("(c p) e -> p c e",
                                                     p=128)))
                qWI.dma(ge.dma_start(wk_sb[:],
                                     wkT_d.rearrange("(c p) e -> p c e",
                                                     p=128)))
                qWI.dma(ge.dma_start(wv_sb[:],
                                     wvT_d.rearrange("(c p) e -> p c e",
                                                     p=128)))
                qWI.dma(ge.dma_start(projn[:],
                                     proj_d.rearrange("(c p) q -> p c q",
                                                      p=128)))

                with contextlib.ExitStack() as ps:
                    csum_ps = ps.enter_context(nc.psum_tensor("csum_ps",
                                                              [1, E], F32))
                    csumB_ps = ps.enter_context(nc.psum_tensor("csumB_ps",
                                                               [128, E], F32))
                    qWI.wait(te)
                    cG.wait(te, m_init_g)
                    for c in range(8):
                        m1 = MM(csum_ps[:], ones_bf[:, 0:1], ct_sb[:, c, :],
                                start=(c == 0), stop=(c == 7), tag=(c == 7))
                    cP.wait(ve, m1)
                    m2 = V(ve.tensor_copy(csum_row[:], csum_ps[:]))
                    cV.wait(te, m2)
                    m3 = MM(csumB_ps[:], ones_f[:], csum_row[:], start=True,
                            stop=True, tag=True)
                    cP.wait(ve, m3)
                    m_csumB = V(ve.tensor_copy(csumB[:], csumB_ps[:]))

                m_x_use_s = [0, 0]
                m_x_use_v = [0, 0]
                m_ufm_use = [0, 0]
                m_kv_ready = [0, 0]
                m_z_ready = [0] * NBT
                m_qp_ready = [0] * NBT
                m_kvadd_done = 0
                m_z = 0
                m_prev_qpkp = 0
                m_prev_tile_pe = 0

                def load_x(g, buf):
                    cS.wait(ge, m_x_use_s[buf])
                    cV.wait(ge, m_x_use_v[buf])
                    return xs1[buf].dma(ge.dma_start(
                        x_sb[buf][:], x_d[g * 128:(g + 1) * 128, :]))

                m_x_ld = [0] * (4 * NBT)
                m_x_ld[0] = load_x(0, 0)

                for bt in range(NBT):
                    buf = bt % 2
                    seq = bt // 4
                    cP.wait(sy, m_ufm_use[buf])
                    m_tr = 0
                    for s in range(4):
                        g = bt * 4 + s
                        xb = g % 2
                        if g + 1 < 4 * NBT:
                            m_x_ld[g + 1] = load_x(g + 1, (g + 1) % 2)
                        xs1[xb].wait(ve, m_x_ld[g])
                        m_rstd, m_nmr = stats(x_sb[xb])
                        ub = s % 2
                        uT[buf].wait(se)
                        cV.wait(se, m_nmr)
                        xs1[xb].wait(se, m_x_ld[g])
                        m_u = Sc(se.activation(u_tm[:, ub, :], x_sb[xb][:],
                                               AF.Identity, bias=nmr_t[:],
                                               scale=rstd_t[:]))
                        m_x_use_s[xb] = m_u
                        m_x_use_v[xb] = cV.n
                        cS.wait(sy, m_u)
                        for dc in range(8):
                            m_tr = uT[buf].dma(sy.dma_start(
                                u_fm[buf][:, dc, s * 128:(s + 1) * 128],
                                u_tm[:, ub, dc * 128:(dc + 1) * 128],
                                transpose=True))
                    with contextlib.ExitStack() as ps:
                        q_ps = [ps.enter_context(nc.psum_tensor(U("q_ps"),
                                                                [128, 512],
                                                                F32))
                                for i in range(2)]
                        k_ps = [ps.enter_context(nc.psum_tensor(U("k_ps"),
                                                                [128, 512],
                                                                F32))
                                for i in range(2)]
                        qp_ps = [ps.enter_context(nc.psum_tensor(U("qp_ps"),
                                                                 [128, 512],
                                                                 F32))
                                 for i in range(2)]
                        kp_ps = ps.enter_context(nc.psum_tensor(U("kp_ps"),
                                                                [128, 4, PD],
                                                                F32))
                        uT[buf].wait(te, m_tr)
                        if bt >= 1:
                            cV.wait(te, m_kvadd_done)
                            cV.wait(te, m_z)
                        m_qev = [0, 0]
                        m_kev = [0, 0]
                        cP.wait(se, m_prev_qpkp)
                        for ec in range(8):
                            r = ec % 2
                            cS.wait(te, m_qev[r])
                            for dc in range(8):
                                m_qmm = MM(q_ps[r][:],
                                           wq_sb[:, dc, ec * 128:(ec + 1) * 128],
                                           u_fm[buf][:, dc, :],
                                           start=(dc == 0), stop=(dc == 7),
                                           tag=(dc == 7))
                            cS.wait(te, m_kev[r])
                            for dc in range(8):
                                m_kmm = MM(k_ps[r][:],
                                           wk_sb[:, dc, ec * 128:(ec + 1) * 128],
                                           u_fm[buf][:, dc, :],
                                           start=(dc == 0), stop=(dc == 7),
                                           tag=(dc == 7))
                            cP.wait(se, m_qmm)
                            m_qev[r] = Sc(se.activation(qev[:, r, :],
                                                        q_ps[r][:], AF.Copy))
                            cP.wait(se, m_kmm)
                            m_kev[r] = Sc(se.activation(kev[:, ec, :],
                                                        k_ps[r][:], AF.Copy))
                            cS.wait(te, m_qev[r])
                            for pc in range(2):
                                m_qpmm = MM(qp_ps[pc][:],
                                            projn[:, ec, pc * 128:(pc + 1) * 128],
                                            qev[:, r, :], start=(ec == 0),
                                            stop=(ec == 7),
                                            tag=(ec == 7 and pc == 1))
                        cS.wait(te, m_kev[0])
                        cS.wait(te, m_kev[1])
                        for s in range(4):
                            for ec in range(8):
                                m_qpkp = MM(kp_ps[:, s, :],
                                            kev[:, ec, s * 128:(s + 1) * 128],
                                            projn[:, ec, :], start=(ec == 0),
                                            stop=(ec == 7),
                                            tag=(s == 3 and ec == 7))
                        m_prev_qpkp = m_qpkp
                        cP.wait(ve, m_qpmm)
                        cP.wait(ve, m_prev_tile_pe)
                        qpv0 = qp_ps[0][:]
                        qpv1 = qp_ps[1][:]
                        V(ve.tensor_scalar(el_t[:, 0:512], qpv0, 0.0, None,
                                           ALU.min))
                        V(ve.tensor_scalar(el_t[:, 512:1024], qpv1, 0.0, None,
                                           ALU.min))
                        V(ve.tensor_scalar(el_r[:, 0:512], qpv0, 0.0, None,
                                           ALU.max))
                        m_e1 = V(ve.tensor_scalar(el_r[:, 512:1024], qpv1, 0.0,
                                                  None, ALU.max))
                        cV.wait(se, m_e1)
                        m_e2 = Sc(se.activation(el_e[:], el_t[:], AF.Exp))
                        cS.wait(ve, m_e2)
                        m_qp = 0
                        for pc in range(2):
                            m_qp = V(ve.scalar_tensor_tensor(
                                qp_fm[:, pc, bt * 512:(bt + 1) * 512],
                                el_r[:, pc * 512:(pc + 1) * 512], 0.0,
                                el_e[:, pc * 512:(pc + 1) * 512], ALU.add,
                                ALU.add))
                        m_qp_ready[bt] = m_qp
                        cP.wait(ve, m_qpkp)
                        kpv = kp_ps[:].rearrange("p s f -> p (s f)")
                        cV.wait(ve, m_qp)
                        V(ve.tensor_scalar(el_t[:], kpv, 0.0, None, ALU.min))
                        m_e3 = V(ve.tensor_scalar(el_r[:], kpv, 0.0, None,
                                                  ALU.max))
                        cV.wait(se, m_e3)
                        m_e4 = Sc(se.activation(el_e[:], el_t[:], AF.Exp))
                        cS.wait(ve, m_e4)
                        m_kp = V(ve.scalar_tensor_tensor(
                            kp_tm[:].rearrange("p s f -> p (s f)"),
                            el_r[:], 0.0, el_e[:], ALU.add, ALU.add))
                    with contextlib.ExitStack() as ps:
                        zs_ps = ps.enter_context(nc.psum_tensor(U("zs_ps"),
                                                                [128, 4], F32))
                        cV.wait(te, m_qp_ready[bt])
                        for s in range(4):
                            for pc in range(2):
                                m_zmm = MM(
                                    zs_ps[:, s:s + 1],
                                    qp_fm[:, pc,
                                          bt * 512 + s * 128:
                                          bt * 512 + (s + 1) * 128],
                                    ones_bf[:, 0:1], start=(pc == 0),
                                    stop=(pc == 1),
                                    tag=(s == 3 and pc == 1))
                        cP.wait(ve, m_zmm)
                        cS.wait(ve, m_e4)
                        m_zt2 = V(ve.tensor_scalar(el_t[:, 0:4], zs_ps[:],
                                                   EPS_Z, None, ALU.add))
                        cV.wait(ve, m_zt2)
                        m_z = V(ve.reciprocal(z_sb[:, bt * 4:(bt + 1) * 4],
                                              el_t[:, 0:4]))
                        m_z_ready[bt] = m_z
                    with contextlib.ExitStack() as ps:
                        v_ps = [ps.enter_context(nc.psum_tensor(U("v_ps"),
                                                                [128, D], F32))
                                for i in range(2)]
                        kv_ps = ps.enter_context(nc.psum_tensor(U("kv_ps"),
                                                                [128, 2, D],
                                                                F32))
                        cV.wait(te, m_z)
                        m_vev = [0, 0]
                        for s in range(4):
                            r = s % 2
                            cS.wait(te, m_vev[r])
                            for dc in range(8):
                                for h in range(2):
                                    m_vmm = MM(
                                        v_ps[r][:, h * 512:(h + 1) * 512],
                                        u_fm[buf][:, dc, s * 128:(s + 1) * 128],
                                        wv_sb[:, dc, h * 512:(h + 1) * 512],
                                        start=(dc == 0), stop=(dc == 7),
                                        tag=(dc == 7 and h == 1))
                            cP.wait(se, m_vmm)
                            m_vev[r] = Sc(se.activation(v_tm[buf][:, s, :],
                                                        v_ps[r][:], AF.Copy))
                            cS.wait(te, m_vev[r])
                            cV.wait(te, m_kp)
                            for pc in range(2):
                                for h in range(2):
                                    m_kvlast = MM(
                                        kv_ps[:, pc, h * 512:(h + 1) * 512],
                                        kp_tm[:, s, pc * 128:(pc + 1) * 128],
                                        v_tm[buf][:, s, h * 512:(h + 1) * 512],
                                        start=(s == 0), stop=(s == 3),
                                        tag=(s == 3 and pc == 1 and h == 1))
                        m_ufm_use[buf] = m_kvlast
                        m_prev_tile_pe = m_kvlast
                        cP.wait(ve, m_kvlast)
                        kvv = kv_ps[:].rearrange("p c d -> p (c d)")
                        kvs = kv_sb[:].rearrange("p c d -> p (c d)")
                        if bt % 4 == 0:
                            m_kvadd_done = V(ve.tensor_copy(kvs, kvv))
                        else:
                            cV.wait(ve, m_kvadd_done)
                            m_kvadd_done = V(ve.tensor_tensor(kvs, kvs, kvv,
                                                              ALU.add))
                        if bt % 4 == 3:
                            cV.wait(ve, m_kvadd_done)
                            m_kv_ready[seq] = V(ve.tensor_copy(
                                kv_bf[:, seq, :, :].rearrange(
                                    "p c d -> p (c d)"), kvs))

            mB_v, mB_s, mB_p = cV.n, cS.n, cP.n
            for eng in (sy, se, ve, ge):
                cV.wait(eng, mB_v)
                cS.wait(eng, mB_s)
                cP.wait(eng, mB_p)
                for qq in (xs1[0], xs1[1], uT[0], uT[1], qZ, qWI):
                    qq.wait(eng)

            p2 = contextlib.ExitStack()
            with p2:
                fn_res = p2.enter_context(nc.sbuf_tensor("fn_res",
                                                         [128, NT, D], BF16))
                x2_sb = [p2.enter_context(nc.sbuf_tensor(f"x2_sb{i}",
                                                         [128, D], F32))
                         for i in range(2)]
                ao_sb = p2.enter_context(nc.sbuf_tensor("ao_sb", [128, D],
                                                        F32))
                t2_sb = p2.enter_context(nc.sbuf_tensor("t2_sb", [128, D],
                                                        F32))
                x1_sb = [p2.enter_context(nc.sbuf_tensor(f"x1_sb{i}",
                                                         [128, D], F32))
                         for i in range(2)]
                fe_sb = p2.enter_context(nc.sbuf_tensor("fe_sb", [128, D],
                                                        F32))
                fnT_sb = [p2.enter_context(nc.sbuf_tensor(f"fnT_sb{i}",
                                                          [128, 8, 128],
                                                          BF16))
                          for i in range(2)]
                sc_sb = p2.enter_context(nc.sbuf_tensor("sc_sb", [128, E],
                                                        F32))
                sc2_sb = p2.enter_context(nc.sbuf_tensor("sc2_sb", [128, E],
                                                         F32))
                mx8 = p2.enter_context(nc.sbuf_tensor("mx8", [128, 8], F32))
                mi8 = p2.enter_context(nc.sbuf_tensor("mi8", [128, 8], U32))
                midx_f = p2.enter_context(nc.sbuf_tensor("midx_f", [128, 1],
                                                         F32))
                oh_bf = p2.enter_context(nc.sbuf_tensor("oh_bf", [128, 8],
                                                        BF16))
                ofsB = p2.enter_context(nc.sbuf_tensor("ofsB", [128, 8], F32))
                t1_sb = p2.enter_context(nc.sbuf_tensor("t1_sb", [128, 8],
                                                        F32))
                ea_sb = p2.enter_context(nc.sbuf_tensor("ea_sb", [128, 1],
                                                        F32))
                on1 = p2.enter_context(nc.sbuf_tensor("on1", [128, 1], F32))
                rstd2f = p2.enter_context(nc.sbuf_tensor("rstd2f", [128, 1],
                                                         F32))
                nmr2f = p2.enter_context(nc.sbuf_tensor("nmr2f", [128, 1],
                                                        F32))
                mu2f = p2.enter_context(nc.sbuf_tensor("mu2f", [128, 1], F32))
                ir2f = p2.enter_context(nc.sbuf_tensor("ir2f", [128, 1], F32))
                n_col_sb = p2.enter_context(nc.sbuf_tensor("n_col_sb", [8, 1],
                                                           F32))
                cmp8 = p2.enter_context(nc.sbuf_tensor("cmp8", [8, 8], F32))
                r8 = p2.enter_context(nc.sbuf_tensor("r8", [8, 1], F32))
                c_al_sb = p2.enter_context(nc.sbuf_tensor("c_al_sb", [8, 1],
                                                          F32))
                B_col_sb = p2.enter_context(nc.sbuf_tensor("B_col_sb", [8, 1],
                                                           F32))
                cmp2_sb = p2.enter_context(nc.sbuf_tensor("cmp2_sb", [8, NST],
                                                          F32))
                st_e_sb = p2.enter_context(nc.sbuf_tensor("st_e_sb", [1, NST],
                                                          F32))
                st_eB_sb = p2.enter_context(nc.sbuf_tensor("st_eB_sb",
                                                           [128, NST], F32))
                Brow32_sb = p2.enter_context(nc.sbuf_tensor("Brow32_sb",
                                                            [1, 256], F32))
                prod_sb = p2.enter_context(nc.sbuf_tensor("prod_sb",
                                                          [128, NT, E], F32))
                red_sb = p2.enter_context(nc.sbuf_tensor("red_sb", [128, NT],
                                                         F32))
                destf_sb = p2.enter_context(nc.sbuf_tensor("destf_sb",
                                                           [128, NT], F32))
                ohmat_sb = p2.enter_context(nc.sbuf_tensor("ohmat_sb",
                                                           [8, NST], F32))
                b2all_sb = p2.enter_context(nc.sbuf_tensor("b2all_sb",
                                                           [NST, D], F32))
                w1off_f = p2.enter_context(nc.sbuf_tensor("w1off_f",
                                                          [128, 8, NST], F32))
                w2off_f = p2.enter_context(nc.sbuf_tensor("w2off_f",
                                                          [128, 32, NST],
                                                          F32))
                imod8 = p2.enter_context(nc.sbuf_tensor("imod8", [8, 256],
                                                        F32))
                I8rep = p2.enter_context(nc.sbuf_tensor("I8rep", [8, 256],
                                                        F32))
                G_pm = p2.enter_context(nc.sbuf_tensor("G_pm", [E, D], F32))
                BT_pm = p2.enter_context(nc.sbuf_tensor("BT_pm", [E, D], F32))
                B1_pm = p2.enter_context(nc.sbuf_tensor("B1_pm", [E, FF],
                                                        F32))
                B2_pm = p2.enter_context(nc.sbuf_tensor("B2_pm", [E, D], F32))

                qG.dma(ge.dma_start(G_pm[:], gam_d))
                qG.dma(ge.dma_start(BT_pm[:], bet_d))
                qG.dma(ge.dma_start(B1_pm[:], b1r_d))
                qG.dma(ge.dma_start(B2_pm[:], b2r_d))
                m_gb = qG.n
                m_imod = cG.inc(ge.iota(
                    imod8[:].rearrange("p (a b) -> p a b", b=8),
                    [[0, 32], [1, 8]], base=0, channel_multiplier=0,
                    allow_small_or_imprecise_dtypes=True))
                cG.wait(ve, m_imod)
                m_i8rep = V(ve.tensor_scalar(I8rep[:], imod8[:],
                                             iota_p[0:8, 0:1], None,
                                             ALU.is_equal))

                with contextlib.ExitStack() as ps:
                    num_ps = [ps.enter_context(nc.psum_tensor(U("num_ps"),
                                                              [128, D], F32))
                              for i in range(2)]
                    sc_psF = ps.enter_context(nc.psum_tensor("sc_ps",
                                                             [128, 512], F32))
                    pos_psF = ps.enter_context(nc.psum_tensor("pos_ps",
                                                              [128, 512],
                                                              F32))
                    cnt_psF = ps.enter_context(nc.psum_tensor("cnt_ps",
                                                              [1, 512], F32))
                    baseB_psF = ps.enter_context(nc.psum_tensor("baseB_ps",
                                                                [128, 512],
                                                                F32))
                    sc_ps = sc_psF[:, 0:E]
                    pos_ps = pos_psF[:, 0:E]
                    cnt_ps = cnt_psF[:, 0:E]
                    baseB_ps = baseB_psF[:, 0:E]

                    m_x2_use = [0, 0]
                    m_x1_use_s = [0, 0]
                    m_x1_st = [0, 0]
                    m_fnT_use = [0, 0]
                    m_num_ev = [0, 0]
                    m_base_ev = 0
                    m_base_ev_is_g = True
                    m_sc_ev = 0
                    m_ofs_used = 0
                    m_pos_used = 0
                    m_pos_prev = 0

                    def load_x2(t, b):
                        cV.wait(ge, m_x2_use[b])
                        return xs2[b].dma(ge.dma_start(
                            x2_sb[b][:], x_d[t * 128:(t + 1) * 128, :]))

                    m_x2_ld = [0] * NT
                    m_x2_ld[0] = load_x2(0, 0)

                    for t in range(NT):
                        b = t % 2
                        seq = t // 16
                        if t + 1 < NT:
                            m_x2_ld[t + 1] = load_x2(t + 1, (t + 1) % 2)
                        if t == 0:
                            cV.wait(te, m_kv_ready[1])
                        cV.wait(te, m_kv_ready[seq])
                        cS.wait(te, m_num_ev[b])
                        for pc in range(2):
                            for h in range(2):
                                m_num = MM(num_ps[b][:, h * 512:(h + 1) * 512],
                                           qp_fm[:, pc, t * 128:(t + 1) * 128],
                                           kv_bf[:, seq, pc,
                                                 h * 512:(h + 1) * 512],
                                           start=(pc == 0), stop=(pc == 1),
                                           tag=(pc == 1 and h == 1))
                        cP.wait(se, m_num)
                        cV.wait(se, m_z_ready[t // 4])
                        m_ao = Sc(se.activation(ao_sb[:], num_ps[b][:],
                                                AF.Copy,
                                                scale=z_sb[:, t:t + 1]))
                        m_num_ev[b] = m_ao
                        cS.wait(ve, m_ao)
                        m_rstd2, m_nmr2 = stats(ao_sb)
                        cV.wait(se, m_nmr2)
                        m_t2 = Sc(se.activation(t2_sb[:], ao_sb[:],
                                                AF.Identity, bias=nmr_t[:],
                                                scale=rstd_t[:]))
                        xs2[b].wait(ve, m_x2_ld[t])
                        cS.wait(ve, m_t2)
                        qX1[b].wait(ve, m_x1_st[b])
                        cS.wait(ve, m_x1_use_s[b])
                        m_x1 = V(ve.tensor_tensor(x1_sb[b][:], x2_sb[b][:],
                                                  t2_sb[:], ALU.add))
                        m_x2_use[b] = m_x1
                        cV.wait(ge, m_x1)
                        m_x1_st[b] = qX1[b].dma(
                            ge.dma_start(x1_d[t * 128:(t + 1) * 128, :],
                                         x1_sb[b][:]))
                        m_rstd3, m_nmr3 = stats(x1_sb[b],
                                                rstd_out=s3_sb[:, t:t + 1],
                                                nmr_out=b3_sb[:, t:t + 1])
                        cV.wait(se, m_nmr3)
                        m_fe = Sc(se.activation(fe_sb[:], x1_sb[b][:],
                                                AF.Identity,
                                                bias=b3_sb[:, t:t + 1],
                                                scale=s3_sb[:, t:t + 1]))
                        m_x1_use_s[b] = m_fe
                        cS.wait(ve, m_fe)
                        m_rstdi, m_nmri = stats(fe_sb, rstd_out=rstd2f[:],
                                                nmr_out=nmr2f[:],
                                                mu_out=mu2f[:])
                        cS.wait(ve, m_rstdi)
                        m_ir = V(ve.reciprocal(ir2f[:], rstd2f[:]))
                        cV.wait(se, m_nmri)
                        m_fn = Sc(se.activation(fn_res[:, t, :], fe_sb[:],
                                                AF.Identity, bias=nmr2f[:],
                                                scale=rstd2f[:]))
                        cS.wait(sy, m_fn)
                        cP.wait(sy, m_fnT_use[b])
                        m_fnT = 0
                        for dc in range(8):
                            m_fnT = fT[b].dma(sy.dma_start(
                                fnT_sb[b][:, dc, :],
                                fn_res[:, t, dc * 128:(dc + 1) * 128],
                                transpose=True))
                        fT[b].wait(te, m_fnT)
                        cV.wait(te, m_sc_ev)
                        for dc in range(8):
                            m_sc_last = MM(sc_ps, fnT_sb[b][:, dc, :],
                                           ct_sb[:, dc, :],
                                           start=(dc == 0), stop=(dc == 7),
                                           tag=(dc == 7))
                        m_fnT_use[b] = m_sc_last
                        cP.wait(ve, m_sc_last)
                        cV.wait(ve, m_ir)
                        m_sc_ev = V(ve.tensor_scalar(sc_sb[:], sc_ps, ir2f[:],
                                                     None, ALU.mult))
                        cV.wait(ve, m_sc_ev)
                        m_sc2 = V(ve.scalar_tensor_tensor(sc2_sb[:], csumB[:],
                                                          mu2f[:], sc_sb[:],
                                                          ALU.mult, ALU.add))
                        cV.wait(ve, m_sc2)
                        m_mx = V(ve.max(mx8[:], sc2_sb[:]))
                        cV.wait(ve, m_mx)
                        m_mi = V(ve.max_index(mi8[:], mx8[:], sc2_sb[:]))
                        cV.wait(se, m_mx)
                        m_ea = Sc(se.activation(ea_sb[:], mx8[:, 0:1], AF.Exp,
                                                scale=-1.0))
                        cS.wait(ve, m_ea)
                        m_o1 = V(ve.tensor_scalar(on1[:], ea_sb[:], 1.0, None,
                                                  ALU.add))
                        cV.wait(ve, m_o1)
                        m_al = V(ve.reciprocal(alpha_sb[:, t:t + 1], on1[:]))
                        cV.wait(ve, m_mi)
                        cP.wait(ve, m_pos_prev)
                        m_mif = V(ve.tensor_copy(midx_f[:], mi8[:, 0:1]))
                        cV.wait(ve, m_mif)
                        m_oh = V(ve.tensor_scalar(oh_all[:, t, :], iota8[:],
                                                  midx_f[:], None,
                                                  ALU.is_equal))
                        cV.wait(ve, m_oh)
                        m_ohb = V(ve.tensor_copy(oh_bf[:], oh_all[:, t, :]))
                        if m_base_ev_is_g:
                            cG.wait(te, m_init_g)
                        else:
                            cV.wait(te, m_base_ev)
                        cV.wait(te, m_ofs_used)
                        m_bb = MM(baseB_ps, ones_f[:], base_row[:], start=True,
                                  stop=True, tag=True)
                        cP.wait(ve, m_bb)
                        m_ofs = V(ve.tensor_tensor(ofsB[:], baseB_ps,
                                                   mione[:], ALU.add))
                        m_ofs_used = m_ofs
                        cV.wait(te, m_ohb)
                        cV.wait(te, m_pos_used)
                        MM(pos_ps, tri_bf[:], oh_bf[:], start=True, stop=True,
                           tag=False)
                        m_pos = MM(cnt_ps, ones_bf[:, 0:1], oh_bf[:],
                                   start=True, stop=True, tag=True)
                        m_pos_prev = m_pos
                        cP.wait(ve, m_pos)
                        cV.wait(ve)
                        m_base_ev = V(ve.tensor_tensor(base_row[:],
                                                       base_row[:], cnt_ps,
                                                       ALU.add))
                        m_base_ev_is_g = False
                        cV.wait(ve, m_ofs)
                        m_t1 = V(ve.tensor_tensor(t1_sb[:], pos_ps, ofsB[:],
                                                  ALU.add))
                        m_pos_used = m_t1
                        cV.wait(ve, m_t1)
                        m_t1b = V(ve.tensor_tensor(t1_sb[:], t1_sb[:],
                                                   oh_all[:, t, :], ALU.mult))
                        cV.wait(ve, m_t1b)
                        V(ve.reduce_sum(pos_all[:, t:t + 1], t1_sb[:], AX.X))

                    m_fn_all = cS.n

                with contextlib.ExitStack() as ps:
                    ps_cnt = ps.enter_context(nc.psum_tensor("f_cnt", [8, 1],
                                                             F32))
                    ps_pre = ps.enter_context(nc.psum_tensor("f_pre", [8, 1],
                                                             F32))
                    ps_ste = ps.enter_context(nc.psum_tensor("f_ste",
                                                             [1, NST], F32))
                    ps_steB = ps.enter_context(nc.psum_tensor("f_steB",
                                                              [128, NST],
                                                              F32))
                    ps_brd = ps.enter_context(nc.psum_tensor("f_brd",
                                                             [1, 256], F32))
                    ps_bB = ps.enter_context(nc.psum_tensor("f_bB",
                                                            [128, 256], F32))
                    mRF_v, mRF_s = cV.n, cS.n
                    cV.wait(te, mRF_v)
                    cS.wait(te, mRF_s)
                    cG.wait(te)
                    cV.wait(ve, mRF_v)
                    cS.wait(ve, mRF_s)
                    m1 = MM(ps_cnt[:], base_row[:], ones_f[0:1, 0:1],
                            start=True, stop=True, tag=True)
                    cP.wait(ve, m1)
                    e1 = V(ve.tensor_copy(n_col_sb[:], ps_cnt[:]))
                    cV.wait(ve, e1)
                    e2 = V(ve.tensor_scalar(cmp8[:], thrT[:, 0:8],
                                            n_col_sb[:], None, ALU.is_lt))
                    cV.wait(ve, e2)
                    e3 = V(ve.reduce_sum(r8[:], cmp8[:], AX.X))
                    cV.wait(ve, e3)
                    e4 = V(ve.tensor_scalar(c_al_sb[:], r8[:], float(A), None,
                                            ALU.mult))
                    cV.wait(te, e4)
                    m2 = MM(ps_pre[:], sU8[:], c_al_sb[:], start=True,
                            stop=True, tag=True)
                    cP.wait(ve, m2)
                    e5 = V(ve.tensor_copy(B_col_sb[:], ps_pre[:]))
                    cV.wait(ve, e5)
                    e6 = V(ve.tensor_scalar(cmp2_sb[:], thrT[:, 0:NST],
                                            B_col_sb[:], None, ALU.is_ge))
                    cV.wait(te, e6)
                    m3 = MM(ps_ste[:], ones8[:], cmp2_sb[:], start=True,
                            stop=True, tag=True)
                    cP.wait(ve, m3)
                    e7 = V(ve.tensor_scalar(st_e_sb[:], ps_ste[:], -1.0,
                                            None, ALU.add))
                    cV.wait(te, e7)
                    m4 = MM(ps_steB[:], ones_f[:], st_e_sb[:],
                            start=True, stop=True, tag=True)
                    cP.wait(ve, m4)
                    e8 = V(ve.tensor_copy(st_eB_sb[:], ps_steB[:]))
                    m5 = MM(ps_brd[:], B_col_sb[:], I8rep[:], start=True,
                            stop=True, tag=True)
                    cP.wait(ve, m5)
                    e9 = V(ve.tensor_copy(Brow32_sb[:], ps_brd[:]))
                    cV.wait(te, e9)
                    m6 = MM(ps_bB[:], ones_f[:], Brow32_sb[:],
                            start=True, stop=True, tag=True)
                    cP.wait(ve, m6)
                    e10 = V(ve.tensor_tensor(
                        prod_sb[:].rearrange("p t e -> p (t e)"),
                        oh_all[:].rearrange("p t e -> p (t e)"),
                        ps_bB[:], ALU.mult))
                    cV.wait(ve, e10)
                    e11 = V(ve.reduce_sum(red_sb[:], prod_sb[:], AX.X))
                    cV.wait(ve, e11)
                    e12 = V(ve.tensor_tensor(destf_sb[:], red_sb[:],
                                             pos_all[:], ALU.add))
                    cV.wait(ve, e12)
                    m_dest_all = V(ve.tensor_copy(dest_sb[:], destf_sb[:]))
                    cV.wait(ve, e8)
                    eo = 0
                    for dc in range(8):
                        eo = V(ve.tensor_scalar(w1off_f[:, dc, :],
                                                st_eB_sb[:], float(D),
                                                iota_pdc[:, dc:dc + 1],
                                                ALU.mult, ALU.add))
                    for fc in range(32):
                        eo = V(ve.tensor_scalar(w2off_f[:, fc, :],
                                                st_eB_sb[:], float(FF),
                                                iota_pfc[:, fc:fc + 1],
                                                ALU.mult, ALU.add))
                    cV.wait(ve, eo)
                    V(ve.tensor_copy(
                        w1off_u[:].rearrange("p a b -> p (a b)"),
                        w1off_f[:].rearrange("p a b -> p (a b)")))
                    m_offs_ready = V(ve.tensor_copy(
                        w2off_u[:].rearrange("p a b -> p (a b)"),
                        w2off_f[:].rearrange("p a b -> p (a b)")))
                    m_f1_done = cV.n
                with contextlib.ExitStack() as ps:
                    ps_ohm = ps.enter_context(nc.psum_tensor("f_ohm",
                                                             [8, NST], F32))
                    ps_b2 = [ps.enter_context(nc.psum_tensor(f"f_b2{h}",
                                                             [NST, 512], F32))
                             for h in range(2)]
                    ps_sel = [ps.enter_context(nc.psum_tensor(f"f_sel{i}",
                                                              [128, NST],
                                                              F32))
                              for i in range(2)]
                    cV.wait(te, m_f1_done)
                    m7 = MM(ps_ohm[:], D8[:], cmp2_sb[:], start=True,
                            stop=True, tag=True)
                    cP.wait(ve, m7)
                    e14 = V(ve.tensor_copy(ohmat_sb[:], ps_ohm[:]))
                    qG.wait(te, m_gb)
                    cV.wait(te, e14)
                    for h in range(2):
                        m8 = MM(ps_b2[h][:], ohmat_sb[:],
                                B2_pm[:, h * 512:(h + 1) * 512], start=True,
                                stop=True, tag=(h == 1))
                    cP.wait(ve, m8)
                    V(ve.tensor_copy(b2all_sb[:, 0:512], ps_b2[0][:]))
                    e15 = V(ve.tensor_copy(b2all_sb[:, 512:1024],
                                           ps_b2[1][:]))
                    cV.wait(sy, e15)
                    qB.dma(sy.dma_start(b2scr_2d, b2all_sb[:]))
                    m_sel_ev = [0, 0]
                    k = 0
                    for pm, dall, nch in ((G_pm, g_all, 8), (BT_pm, bt_all, 8),
                                          (B1_pm, b1_all, 32)):
                        for c in range(nch):
                            r = k % 2
                            cV.wait(te, m_sel_ev[r])
                            msel = MM(ps_sel[r][:],
                                      pm[:, c * 128:(c + 1) * 128],
                                      ohmat_sb[:], start=True, stop=True,
                                      tag=True)
                            cP.wait(ve, msel)
                            m_sel_ev[r] = V(ve.tensor_copy(dall[:, c, :],
                                                           ps_sel[r][:]))
                            k += 1
                    m_gall = cV.n

                cV.wait(ge, m_dest_all)
                cS.wait(ge, m_fn_all)
                qZ.wait(ge)
                for t in range(NT):
                    qIs[t % 2].dma(ge.indirect_dma_start(
                        fn_tbl,
                        bass.IndirectOffsetOnAxis(ap=dest_sb[:, t:t + 1],
                                                  axis=0),
                        fn_res[:, t, :], None, bounds_check=None))

        mC_v, mC_s, mC_p = cV.n, cS.n, cP.n
        for eng in (te, ve, se, sy, ge):
            cV.wait(eng, mC_v)
            cS.wait(eng, mC_s)
            cP.wait(eng, mC_p)
            for qq in (qIs[0], qIs[1], qX1[0], qX1[1], xs2[0], xs2[1],
                       fT[0], fT[1], qB):
                qq.wait(eng)

        ff = contextlib.ExitStack()
        with ff:
            w1_sb = ff.enter_context(nc.sbuf_tensor("w1_sb", [128, 8, FF],
                                                    BF16))
            w2_sb = ff.enter_context(nc.sbuf_tensor("w2_sb", [128, 32, D],
                                                    BF16))
            hr_sb = ff.enter_context(nc.sbuf_tensor("hr_sb", [128, 8, 512],
                                                    BF16))
            h_sb = ff.enter_context(nc.sbuf_tensor("h_sb", [128, 8, 512],
                                                   BF16))
            a1r = ff.enter_context(nc.sbuf_tensor("a1r", [128, 32, 512],
                                                  BF16))
            y2_sb = ff.enter_context(nc.sbuf_tensor("y2_sb", [128, D], F32))
            b2B = ff.enter_context(nc.sbuf_tensor("b2B", [128, D], F32))
            b2row = [ff.enter_context(nc.sbuf_tensor(f"b2row{i}", [1, D],
                                                     F32))
                     for i in range(2)]

            with contextlib.ExitStack() as ps:
                a1_ps = [ps.enter_context(nc.psum_tensor(U("a1_ps"),
                                                         [128, 512], F32))
                         for i in range(2)]
                y2_ps = [ps.enter_context(nc.psum_tensor(U("y2_ps"), [128, D],
                                                         F32))
                         for i in range(2)]
                b2B_ps = ps.enter_context(nc.psum_tensor("b2B_ps", [128, D],
                                                         F32))

                m_w1_use = 0
                m_w2_use = 0
                m_h_use = 0
                m_hr_free = 0
                m_a1_ev_s = [0, 0]
                m_a1_ev_v = [0, 0]
                m_y2_ev = [0, 0]
                m_y2_st = 0
                m_b2B_use = 0
                m_b2B = 0
                m_b2mm_l = [0, 0]
                m_b2ld = [0, 0]

                for s in range(NST):
                    cP.wait(ge, m_w1_use)
                    if s == 0:
                        cV.wait(ge, m_offs_ready)
                    m_w1_ld = 0
                    for dc in range(8):
                        m_w1_ld = qW1.dma(ge.indirect_dma_start(
                            w1_sb[:, dc, :], None, w1_lin,
                            bass.IndirectOffsetOnAxis(
                                ap=w1off_u[:, dc, s:s + 1], axis=0),
                            bounds_check=None))
                    cP.wait(ge, m_w2_use)
                    m_w2_ld = 0
                    for fc in range(32):
                        m_w2_ld = qW2.dma(ge.indirect_dma_start(
                            w2_sb[:, fc, :], None, w2_lin,
                            bass.IndirectOffsetOnAxis(
                                ap=w2off_u[:, fc, s:s + 1], axis=0),
                            bounds_check=None))
                    bb = s % 2
                    cP.wait(sy, m_b2mm_l[bb])
                    if s == 0:
                        qB.wait(sy)
                    m_b2ld[bb] = qB.dma(sy.dma_start(b2row[bb][:],
                                                     b2scr_2d[s:s + 1, :]))
                    cV.wait(te, m_b2B_use)
                    qB.wait(te, m_b2ld[bb])
                    for h in range(2):
                        m_b2mm = MM(b2B_ps[:, h * 512:(h + 1) * 512],
                                    ones_f[:],
                                    b2row[bb][:, h * 512:(h + 1) * 512],
                                    start=True, stop=True, tag=(h == 1))
                    m_b2mm_l[bb] = m_b2mm
                    cP.wait(ve, m_b2mm)
                    m_b2B = V(ve.tensor_copy(b2B[:], b2B_ps[:]))
                    cP.wait(se, m_h_use)
                    if s == 0:
                        cV.wait(se, m_gall)
                    cS.wait(sy, m_hr_free)
                    m_htr = 0
                    for dc in range(8):
                        m_htr = hT.dma(sy.dma_start(
                            hr_sb[:, dc, :],
                            fn_tbl[s * 512:(s + 1) * 512,
                                   dc * 128:(dc + 1) * 128],
                            transpose=True))
                    hT.wait(se, m_htr)
                    m_h = 0
                    for dc in range(8):
                        m_h = Sc(se.activation(h_sb[:, dc, :],
                                               hr_sb[:, dc, :],
                                               AF.Identity,
                                               bias=bt_all[:, dc, s:s + 1],
                                               scale=g_all[:, dc, s:s + 1]))
                    m_hr_free = m_h
                    qW1.wait(te, m_w1_ld)
                    cS.wait(te, m_h)
                    for fc in range(32):
                        ar = fc % 2
                        cS.wait(te, m_a1_ev_s[ar])
                        cV.wait(te, m_a1_ev_v[ar])
                        for dc in range(8):
                            m_a1mm = MM(a1_ps[ar][:],
                                        w1_sb[:, dc, fc * 128:(fc + 1) * 128],
                                        h_sb[:, dc, :],
                                        start=(dc == 0), stop=(dc == 7),
                                        tag=(dc == 7))
                        if fc % 2 == 0:
                            cP.wait(se, m_a1mm)
                            m_a1_ev_s[ar] = Sc(se.activation(
                                a1r[:, fc, :], a1_ps[ar][:], AF.Relu,
                                bias=b1_all[:, fc, s:s + 1]))
                        else:
                            cP.wait(ve, m_a1mm)
                            m_a1_ev_v[ar] = V(ve.tensor_scalar(
                                a1r[:, fc, :], a1_ps[ar][:],
                                b1_all[:, fc, s:s + 1], 0.0,
                                ALU.add, ALU.max))
                    m_w1_use = cP.n
                    m_h_use = cP.n
                    qW2.wait(te, m_w2_ld)
                    m_s1s = cS.n
                    m_s1v = cV.n
                    cS.wait(te, m_s1s)
                    cV.wait(te, m_s1v)
                    for sub in range(4):
                        yr = sub % 2
                        cV.wait(te, m_y2_ev[yr])
                        for fc in range(32):
                            for h in range(2):
                                m_y2mm = MM(
                                    y2_ps[yr][:, h * 512:(h + 1) * 512],
                                    a1r[:, fc, sub * 128:(sub + 1) * 128],
                                    w2_sb[:, fc, h * 512:(h + 1) * 512],
                                    start=(fc == 0), stop=(fc == 31),
                                    tag=(fc == 31 and h == 1))
                        cP.wait(ve, m_y2mm)
                        cV.wait(ve, m_b2B)
                        qY2.wait(ve, m_y2_st)
                        m_y2_ev[yr] = V(ve.scalar_tensor_tensor(
                            y2_sb[:], y2_ps[yr][:], 0.0, b2B[:],
                            ALU.add, ALU.add))
                        cV.wait(sy, m_y2_ev[yr])
                        m_y2_st = qY2.dma(sy.dma_start(
                            y2_tbl[s * 512 + sub * 128:
                                   s * 512 + (sub + 1) * 128, :],
                            y2_sb[:]))
                    m_w2_use = cP.n
                    m_b2B_use = cV.n

        mD_v, mD_s, mD_p = cV.n, cS.n, cP.n
        for eng in (ge, sy, se, ve):
            cV.wait(eng, mD_v)
            cS.wait(eng, mD_s)
            cP.wait(eng, mD_p)
            for qq in (qY2, qW1, qW2, hT, qB):
                qq.wait(eng)

        cb = contextlib.ExitStack()
        with cb:
            y2g = [cb.enter_context(nc.sbuf_tensor(f"y2g{i}", [128, D], F32))
                   for i in range(2)]
            x1c = [cb.enter_context(nc.sbuf_tensor(f"x1c{i}", [128, D], F32))
                   for i in range(2)]
            fe2 = cb.enter_context(nc.sbuf_tensor("fe2", [128, D], F32))
            of_sb = cb.enter_context(nc.sbuf_tensor("of_sb", [128, D], F32))
            t5_sb = cb.enter_context(nc.sbuf_tensor("t5_sb", [128, D], F32))
            xo_sb = [cb.enter_context(nc.sbuf_tensor(f"xo_sb{i}", [128, D],
                                                     F32))
                     for i in range(2)]
            bn6b = cb.enter_context(nc.sbuf_tensor("bn6b", [128, 12], F32))
            mv2b = cb.enter_context(nc.sbuf_tensor("mv2b", [128, 2], F32))
            varb = cb.enter_context(nc.sbuf_tensor("varb", [128, 1], F32))
            lnvb = cb.enter_context(nc.sbuf_tensor("lnvb", [128, 1], F32))
            rstdb = cb.enter_context(nc.sbuf_tensor("rstdb", [128, 1], F32))
            nmrb = cb.enter_context(nc.sbuf_tensor("nmrb", [128, 1], F32))

            st5 = {"se": 0, "ve": 0}

            def stats5(src):
                cS.wait(ve, st5["se"])
                cV.wait(ve)
                V(ve.bn_stats(bn6b[:, 0:6], src[:, 0:512]))
                m1 = V(ve.bn_stats(bn6b[:, 6:12], src[:, 512:1024]))
                cV.wait(ve, m1)
                m2 = V(ve.bn_aggr(mv2b[:], bn6b[:]))
                cV.wait(ve, m2)
                m3 = V(ve.tensor_scalar(varb[:], mv2b[:, 1:2], EPS_LN, None,
                                        ALU.add))
                cV.wait(se, m3)
                m4 = Sc(se.activation(lnvb[:], varb[:], AF.Ln))
                cS.wait(se, m4)
                m5 = Sc(se.activation(rstdb[:], lnvb[:], AF.Exp, scale=-0.5))
                st5["se"] = m5
                cS.wait(ve, m5)
                m6 = V(ve.tensor_scalar(nmrb[:], mv2b[:, 0:1], rstdb[:], -1.0,
                                        ALU.mult, ALU.mult))
                st5["ve"] = m6
                return m5, m6

            m_y2g_use = [0, 0]
            m_x1c_use_s = [0, 0]
            m_x1c_use_v = [0, 0]
            m_xo_st = [0, 0]

            def load_cb(t, b):
                cV.wait(ge, m_y2g_use[b])
                mg = qIg[b].dma(ge.indirect_dma_start(
                    y2g[b][:], None,
                    y2_tbl,
                    bass.IndirectOffsetOnAxis(ap=dest_sb[:, t:t + 1], axis=0),
                    bounds_check=None))
                cS.wait(sy, m_x1c_use_s[b])
                cV.wait(sy, m_x1c_use_v[b])
                mx = qXC[b].dma(sy.dma_start(x1c[b][:],
                                             x1_d[t * 128:(t + 1) * 128, :]))
                return mg, mx

            m_g = [None] * NT
            m_g[0] = load_cb(0, 0)

            for t in range(NT):
                b = t % 2
                if t + 1 < NT:
                    m_g[t + 1] = load_cb(t + 1, (t + 1) % 2)
                mg, mx = m_g[t]
                qXC[b].wait(se, mx)
                m_fe2 = Sc(se.activation(fe2[:], x1c[b][:], AF.Identity,
                                         bias=b3_sb[:, t:t + 1],
                                         scale=s3_sb[:, t:t + 1]))
                qIg[b].wait(ve, mg)
                cS.wait(ve, m_fe2)
                m_of = V(ve.scalar_tensor_tensor(
                    of_sb[:], y2g[b][:], alpha_sb[:, t:t + 1], fe2[:],
                    ALU.mult, ALU.add))
                m_y2g_use[b] = m_of
                cV.wait(ve, m_of)
                m_r5, m_n5 = stats5(of_sb)
                cV.wait(se, m_n5)
                m_t5 = Sc(se.activation(t5_sb[:], of_sb[:], AF.Identity,
                                        bias=nmrb[:], scale=rstdb[:]))
                m_x1c_use_s[b] = m_t5
                cS.wait(ve, m_t5)
                qO[b].wait(ve, m_xo_st[b])
                m_x2o = V(ve.tensor_tensor(xo_sb[b][:], x1c[b][:], t5_sb[:],
                                           ALU.add))
                m_x1c_use_v[b] = m_x2o
                cV.wait(sy, m_x2o)
                m_xo_st[b] = qO[b].dma(sy.dma_start(
                    out_d[t * 128:(t + 1) * 128, :], xo_sb[b][:]))

        qO[0].wait(sy)
        qO[1].wait(sy)
        qIg[0].wait(ge)
        qIg[1].wait(ge)
    return nc


_NC_CACHE = {}


def _get_nc():
    key = 0
    if key not in _NC_CACHE:
        _NC_CACHE[key] = build()
    return _NC_CACHE[key]


def prep_in_maps(inputs):
    x = np.ascontiguousarray(np.asarray(inputs["L0_attn_x"], dtype=np.float32))
    wq = np.asarray(inputs["L0_attn_Wq"], dtype=np.float32)
    wk = np.asarray(inputs["L0_attn_Wk"], dtype=np.float32)
    wv = np.asarray(inputs["L0_attn_Wv"], dtype=np.float32)
    proj = np.ascontiguousarray(np.asarray(inputs["L0_attn_proj"],
                                           dtype=np.float32))
    cent = np.asarray(inputs["L0_ffn_centroids"], dtype=np.float32)
    gam = np.ascontiguousarray(np.asarray(inputs["L0_ffn_gamma"],
                                          dtype=np.float32))
    bet = np.ascontiguousarray(np.asarray(inputs["L0_ffn_beta"],
                                          dtype=np.float32))
    W1 = np.asarray(inputs["L0_ffn_W1"], dtype=np.float32)
    b1 = np.ascontiguousarray(np.asarray(inputs["L0_ffn_b1"],
                                         dtype=np.float32))
    W2 = np.asarray(inputs["L0_ffn_W2"], dtype=np.float32)
    b2 = np.ascontiguousarray(np.asarray(inputs["L0_ffn_b2"],
                                         dtype=np.float32))

    wqT = np.ascontiguousarray(wq.T)
    wkT = np.ascontiguousarray(wk.T)
    wvT = np.ascontiguousarray(wv.T)
    cT = np.ascontiguousarray(cent.T)
    w1T = np.ascontiguousarray(W1.transpose(0, 2, 1))
    w2T = np.ascontiguousarray(W2.transpose(0, 2, 1))

    in_maps = []
    for c in range(NCORE):
        xm = np.ascontiguousarray(x[2 * c:2 * c + 2].reshape(T, D))
        in_maps.append({
            "x": xm, "wqT": wqT, "wkT": wkT, "wvT": wvT, "proj": proj,
            "cT": cT, "gam": gam, "bet": bet, "b1r": b1, "b2r": b2,
            "w1T": w1T, "w2T": w2T,
        })
    return in_maps


def kernel(**inputs):
    in_maps = prep_in_maps(inputs)
    nc = _get_nc()
    res = run_bass_kernel_spmd(nc, in_maps, core_ids=list(range(NCORE)))
    outs = [res.results[c]["out"].reshape(2, 2048, D) for c in range(NCORE)]
    full = np.concatenate(outs, axis=0)
    return full


# version 12 (final)
# speedup vs baseline: 1.2274x; 1.2204x over previous
import contextlib
import numpy as np
import concourse.bass as bass
import concourse.mybir as mybir
from concourse.bass_utils import run_bass_kernel_spmd

F32 = mybir.dt.float32
BF16 = mybir.dt.bfloat16
U32 = mybir.dt.uint32
AF = mybir.ActivationFunctionType
ALU = mybir.AluOpType
AX = mybir.AxisListType

NCORE = 8
D = 1024
PD = 256
FF = 4096
E = 8
T = 4096
NBT = 8
NT = 32
A = 512
NST = 15
SLOTS = NST * A
EPS_LN = 1e-5
EPS_Z = 1e-8


class S:
    def __init__(self, nc, es, name):
        self.h = es.enter_context(nc.semaphore(name))
        self.n = 0

    def dma(self, instr):
        instr.then_inc(self.h, 16)
        self.n += 16
        return self.n

    def inc(self, instr):
        instr.then_inc(self.h, 1)
        self.n += 1
        return self.n

    def wait(self, eng, n=None):
        n = self.n if n is None else n
        if n > 0:
            eng.wait_ge(self.h, n)


def build():
    nc = bass.Bass("TRN2", target_bir_lowering=False)
    _uid = [0]

    def U(name):
        _uid[0] += 1
        return f"{name}_{_uid[0]}"

    te, ve, se, ge, sy = nc.tensor, nc.vector, nc.scalar, nc.gpsimd, nc.sync

    x_d = nc.dram_tensor("x", [T, D], F32, kind="ExternalInput").ap()
    wqr_d = nc.dram_tensor("wqr", [D, D], F32, kind="ExternalInput").ap()
    wkr_d = nc.dram_tensor("wkr", [D, D], F32, kind="ExternalInput").ap()
    wvT_d = nc.dram_tensor("wvT", [D, D], F32, kind="ExternalInput").ap()
    proj_d = nc.dram_tensor("proj", [D, PD], F32, kind="ExternalInput").ap()
    cT_d = nc.dram_tensor("cT", [D, E], F32, kind="ExternalInput").ap()
    gam_d = nc.dram_tensor("gam", [E, D], F32, kind="ExternalInput").ap()
    bet_d = nc.dram_tensor("bet", [E, D], F32, kind="ExternalInput").ap()
    b1r_d = nc.dram_tensor("b1r", [E, FF], F32, kind="ExternalInput").ap()
    b2r_d = nc.dram_tensor("b2r", [E, D], F32, kind="ExternalInput").ap()
    w1T_d = nc.dram_tensor("w1T", [E, D, FF], F32, kind="ExternalInput").ap()
    w2T_d = nc.dram_tensor("w2T", [E, FF, D], F32, kind="ExternalInput").ap()

    out_d = nc.dram_tensor("out", [T, D], F32, kind="ExternalOutput").ap()
    x1_d = nc.dram_tensor("x1d", [T, D], F32, kind="Internal").ap()
    fn_tbl = nc.dram_tensor("fn_tbl", [SLOTS, D], BF16, kind="Internal").ap()
    y2_tbl = nc.dram_tensor("y2_tbl", [SLOTS, D], F32, kind="Internal").ap()
    b2scr_d = nc.dram_tensor("b2scr", [1, NST * D], F32, kind="Internal").ap()
    b2scr_2d = b2scr_d.rearrange("o (s d) -> (o s) d", d=D)

    w1_lin = w1T_d.rearrange("e d f -> (e d) f")
    w2_lin = w2T_d.rearrange("e f d -> (e f) d")

    es = contextlib.ExitStack()
    with es:
        xs1 = [S(nc, es, f"xs1_{i}") for i in range(2)]
        xs2 = [S(nc, es, f"xs2_{i}") for i in range(2)]
        uT = [S(nc, es, f"uT_{i}") for i in range(2)]
        fT = [S(nc, es, f"fT_{i}") for i in range(2)]
        hT = S(nc, es, "hT")
        qWI = S(nc, es, "qWI")
        qW1 = S(nc, es, "qW1")
        qW2 = S(nc, es, "qW2")
        qIs = [S(nc, es, f"qIs_{i}") for i in range(2)]
        qIg = [S(nc, es, f"qIg_{i}") for i in range(2)]
        qX1 = [S(nc, es, f"qX1_{i}") for i in range(2)]
        qXC = [S(nc, es, f"qXC_{i}") for i in range(2)]
        qY2 = S(nc, es, "qY2")
        qO = [S(nc, es, f"qO_{i}") for i in range(2)]
        qB = S(nc, es, "qB")
        qZ = S(nc, es, "qZ")
        qG = S(nc, es, "qG")
        cP = S(nc, es, "cP")
        cV = S(nc, es, "cV")
        cS = S(nc, es, "cS")
        cG = S(nc, es, "cG")

        def V(instr):
            return cV.inc(instr)

        def Sc(instr):
            return cS.inc(instr)

        def MM(out, lhsT, rhs, start, stop, tag=False):
            ins = te.matmul(out, lhsT, rhs, start=start, stop=stop)
            if tag:
                return cP.inc(ins)
            return None

        ones_bf = es.enter_context(nc.sbuf_tensor("ones_bf", [128, 128], BF16))
        ones_f = es.enter_context(nc.sbuf_tensor("ones_f", [1, 128], F32))
        tri_bf = es.enter_context(nc.sbuf_tensor("tri_bf", [128, 128], BF16))
        iota8 = es.enter_context(nc.sbuf_tensor("iota8", [128, 8], F32))
        mione = es.enter_context(nc.sbuf_tensor("mione", [128, 8], F32))
        base_row = es.enter_context(nc.sbuf_tensor("base_row", [1, 8], F32))
        ct_sb = es.enter_context(nc.sbuf_tensor("ct_sb", [128, 8, E], BF16))
        csum_row = es.enter_context(nc.sbuf_tensor("csum_row", [1, E], F32))
        csumB = es.enter_context(nc.sbuf_tensor("csumB", [128, E], F32))
        alpha_sb = es.enter_context(nc.sbuf_tensor("alpha_sb", [128, NT], F32))
        dest_sb = es.enter_context(nc.sbuf_tensor("dest_sb", [128, NT], U32))
        s3_sb = es.enter_context(nc.sbuf_tensor("s3_sb", [128, NT], F32))
        b3_sb = es.enter_context(nc.sbuf_tensor("b3_sb", [128, NT], F32))
        z_sb = es.enter_context(nc.sbuf_tensor("z_sb", [128, NT], F32))
        pos_all = es.enter_context(nc.sbuf_tensor("pos_all", [128, NT], F32))
        oh_all = es.enter_context(nc.sbuf_tensor("oh_all", [128, NT, E], F32))
        iota_p = es.enter_context(nc.sbuf_tensor("iota_p", [128, 1], F32))
        iota_pm1 = es.enter_context(nc.sbuf_tensor("iota_pm1", [128, 1], F32))
        iota_pdc = es.enter_context(nc.sbuf_tensor("iota_pdc", [128, 8], F32))
        iota_pfc = es.enter_context(nc.sbuf_tensor("iota_pfc", [128, 32], F32))
        thrT = es.enter_context(nc.sbuf_tensor("thrT", [8, 16], F32))
        I8 = es.enter_context(nc.sbuf_tensor("I8", [8, 8], F32))
        subD8 = es.enter_context(nc.sbuf_tensor("subD8", [8, 8], F32))
        D8 = es.enter_context(nc.sbuf_tensor("D8", [8, 8], F32))
        sU8 = es.enter_context(nc.sbuf_tensor("sU8", [8, 8], F32))
        ones8 = es.enter_context(nc.sbuf_tensor("ones8", [8, 1], F32))
        g_all = es.enter_context(nc.sbuf_tensor("g_all", [128, 8, NST], F32))
        bt_all = es.enter_context(nc.sbuf_tensor("bt_all", [128, 8, NST], F32))
        b1_all = es.enter_context(nc.sbuf_tensor("b1_all", [128, 32, NST], F32))
        w1off_u = es.enter_context(nc.sbuf_tensor("w1off_u", [128, 8, NST],
                                                  U32))
        w2off_u = es.enter_context(nc.sbuf_tensor("w2off_u", [128, 32, NST],
                                                  U32))

        blk = es.enter_context(nc.Block())

        cG.inc(ge.memset(ones_bf[:], 1.0))
        cG.inc(ge.memset(ones_f[:], 1.0))
        cG.wait(ge)
        cG.inc(ge.affine_select(tri_bf[:], ones_bf[:], [[1, 128]], ALU.is_ge,
                                0.0, base=0, channel_multiplier=-1))
        cG.inc(ge.iota(iota8[:], [[1, 8]], base=0, channel_multiplier=0,
                       allow_small_or_imprecise_dtypes=True))
        cG.inc(ge.memset(mione[:], -1.0))
        cG.inc(ge.memset(ones8[:], 1.0))
        cG.inc(ge.iota(iota_p[:], [[0, 1]], base=0, channel_multiplier=1,
                       allow_small_or_imprecise_dtypes=True))
        cG.inc(ge.iota(iota_pm1[:], [[0, 1]], base=-1, channel_multiplier=1,
                       allow_small_or_imprecise_dtypes=True))
        cG.inc(ge.iota(iota_pdc[:], [[128, 8]], base=0, channel_multiplier=1,
                       allow_small_or_imprecise_dtypes=True))
        cG.inc(ge.iota(iota_pfc[:], [[128, 32]], base=0, channel_multiplier=1,
                       allow_small_or_imprecise_dtypes=True))
        cG.inc(ge.iota(thrT[:], [[A, 16]], base=0, channel_multiplier=0,
                       allow_small_or_imprecise_dtypes=True))
        m_init_g = cG.inc(ge.memset(base_row[:], 0.0))
        cG.wait(ve, m_init_g)
        V(ve.tensor_scalar(I8[:], iota8[0:8, :], iota_p[0:8, 0:1], None,
                           ALU.is_equal))
        V(ve.tensor_scalar(subD8[:], iota8[0:8, :], iota_pm1[0:8, 0:1], None,
                           ALU.is_equal))
        V(ve.tensor_scalar(sU8[:], iota8[0:8, :], iota_p[0:8, 0:1], None,
                           ALU.is_gt))
        cV.wait(ve)
        m_init_v = V(ve.tensor_tensor(D8[:], I8[:], subD8[:], ALU.subtract))
        qWI.dma(ge.dma_start(ct_sb[:], cT_d.rearrange("(c p) e -> p c e",
                                                      p=128)))

        ab = contextlib.ExitStack()
        with ab:
            qp_fm = ab.enter_context(nc.sbuf_tensor("qp_fm", [128, 2, T],
                                                    BF16))
            kv_bf = ab.enter_context(nc.sbuf_tensor("kv_bf", [128, 2, 2, D],
                                                    BF16))
            bn6 = ab.enter_context(nc.sbuf_tensor("bn6", [128, 12], F32))
            mv2 = ab.enter_context(nc.sbuf_tensor("mv2", [128, 2], F32))
            varp = ab.enter_context(nc.sbuf_tensor("varp", [128, 1], F32))
            lnv = ab.enter_context(nc.sbuf_tensor("lnv", [128, 1], F32))
            rstd_t = ab.enter_context(nc.sbuf_tensor("rstd_t", [128, 1], F32))
            nmr_t = ab.enter_context(nc.sbuf_tensor("nmr_t", [128, 1], F32))

            stx = {"se": 0, "ve": 0}

            def stats(src, rstd_out=None, nmr_out=None, mu_out=None):
                rstd_out = rstd_t[:] if rstd_out is None else rstd_out
                nmr_out = nmr_t[:] if nmr_out is None else nmr_out
                cS.wait(ve, stx["se"])
                cV.wait(ve)
                V(ve.bn_stats(bn6[:, 0:6], src[:, 0:512]))
                m1 = V(ve.bn_stats(bn6[:, 6:12], src[:, 512:1024]))
                cV.wait(ve, m1)
                m2 = V(ve.bn_aggr(mv2[:], bn6[:]))
                cV.wait(ve, m2)
                m3 = V(ve.tensor_scalar(varp[:], mv2[:, 1:2], EPS_LN, None,
                                        ALU.add))
                if mu_out is not None:
                    V(ve.tensor_copy(mu_out, mv2[:, 0:1]))
                cV.wait(se, m3)
                m4 = Sc(se.activation(lnv[:], varp[:], AF.Ln))
                cS.wait(se, m4)
                m5 = Sc(se.activation(rstd_out, lnv[:], AF.Exp, scale=-0.5))
                stx["se"] = m5
                cS.wait(ve, m5)
                m6 = V(ve.tensor_scalar(nmr_out, mv2[:, 0:1], rstd_out, -1.0,
                                        ALU.mult, ALU.mult))
                stx["ve"] = m6
                return m5, m6

            at = contextlib.ExitStack()
            with at:
                wqr_sb = at.enter_context(nc.sbuf_tensor("wqr_sb",
                                                         [128, 8, D], BF16))
                wkr_sb = at.enter_context(nc.sbuf_tensor("wkr_sb",
                                                         [128, 8, D], BF16))
                mq_sb = at.enter_context(nc.sbuf_tensor("mq_sb",
                                                        [128, 8, PD], BF16))
                mk_sb = at.enter_context(nc.sbuf_tensor("mk_sb",
                                                        [128, 8, PD], BF16))
                wv_sb = at.enter_context(nc.sbuf_tensor("wv_sb", [128, 8, D],
                                                        BF16))
                projn = at.enter_context(nc.sbuf_tensor("projn", [128, 8, PD],
                                                        BF16))
                x_sb = [at.enter_context(nc.sbuf_tensor(f"x_sb{i}", [128, D],
                                                        F32))
                        for i in range(2)]
                u_tm = at.enter_context(nc.sbuf_tensor("u_tm", [128, 2, D],
                                                       BF16))
                u_fm = [at.enter_context(nc.sbuf_tensor(f"u_fm{i}",
                                                        [128, 8, 512], BF16))
                        for i in range(2)]
                kp_tm = at.enter_context(nc.sbuf_tensor("kp_tm", [128, 4, PD],
                                                        BF16))
                v_tm = [at.enter_context(nc.sbuf_tensor(f"v_tm{i}",
                                                        [128, 4, D], BF16))
                        for i in range(2)]
                zt_sb = at.enter_context(nc.sbuf_tensor("zt_sb",
                                                        [128, 4, 1024], BF16))
                kv_sb = at.enter_context(nc.sbuf_tensor("kv_sb", [128, 2, D],
                                                        F32))
                el_t = at.enter_context(nc.sbuf_tensor("el_t", [128, 1024],
                                                       F32))
                el_e = at.enter_context(nc.sbuf_tensor("el_e", [128, 1024],
                                                       F32))
                el_r = at.enter_context(nc.sbuf_tensor("el_r", [128, 1024],
                                                       F32))

                m_zt = cG.inc(ge.memset(zt_sb[:], 0.0))
                cG.wait(sy, m_zt)
                for i in range(SLOTS // 512):
                    qZ.dma(sy.dma_start(
                        fn_tbl[i * 512:(i + 1) * 512, :].rearrange(
                            "(p a) f -> p a f", p=128),
                        zt_sb[:]))
                qWI.dma(ge.dma_start(wqr_sb[:],
                                     wqr_d.rearrange("(c p) d -> p c d",
                                                     p=128)))
                qWI.dma(ge.dma_start(wkr_sb[:],
                                     wkr_d.rearrange("(c p) d -> p c d",
                                                     p=128)))
                qWI.dma(ge.dma_start(wv_sb[:],
                                     wvT_d.rearrange("(c p) e -> p c e",
                                                     p=128)))
                qWI.dma(ge.dma_start(projn[:],
                                     proj_d.rearrange("(c p) q -> p c q",
                                                      p=128)))

                with contextlib.ExitStack() as ps:
                    mqk_ps = [ps.enter_context(nc.psum_tensor(U("mqk_ps"),
                                                              [128, PD], F32))
                              for i in range(2)]
                    qWI.wait(te)
                    cG.wait(te, m_init_g)
                    m_mev = [0, 0]
                    k2 = 0
                    for wsrc, mdst in ((wqr_sb, mq_sb), (wkr_sb, mk_sb)):
                        for dc in range(8):
                            r = k2 % 2
                            cS.wait(te, m_mev[r])
                            for ec in range(8):
                                m_mm = MM(mqk_ps[r][:],
                                          wsrc[:, ec, dc * 128:(dc + 1) * 128],
                                          projn[:, ec, :], start=(ec == 0),
                                          stop=(ec == 7), tag=(ec == 7))
                            cP.wait(se, m_mm)
                            m_mev[r] = Sc(se.activation(mdst[:, dc, :],
                                                        mqk_ps[r][:],
                                                        AF.Copy))
                            k2 += 1
                    m_mqk_s = cS.n

                m_x_use_s = [0, 0]
                m_x_use_v = [0, 0]
                m_ufm_use = [0, 0]
                m_kv_ready = [0, 0]
                m_z_ready = [0] * NBT
                m_qp_ready = [0] * NBT
                m_kvadd_done = 0
                m_z = 0
                m_prev_tile_pe = 0

                def load_x(g, buf):
                    cS.wait(sy, m_x_use_s[buf])
                    cV.wait(sy, m_x_use_v[buf])
                    return xs1[buf].dma(sy.dma_start(
                        x_sb[buf][:], x_d[g * 128:(g + 1) * 128, :]))

                m_x_ld = [0] * (4 * NBT)
                m_x_ld[0] = load_x(0, 0)

                for bt in range(NBT):
                    buf = bt % 2
                    seq = bt // 4
                    cP.wait(se, m_ufm_use[buf])
                    m_tr = 0
                    for s in range(4):
                        g = bt * 4 + s
                        xb = g % 2
                        if g + 1 < 4 * NBT:
                            m_x_ld[g + 1] = load_x(g + 1, (g + 1) % 2)
                        xs1[xb].wait(ve, m_x_ld[g])
                        m_rstd, m_nmr = stats(x_sb[xb])
                        ub = s % 2
                        uT[buf].wait(se)
                        cV.wait(se, m_nmr)
                        xs1[xb].wait(se, m_x_ld[g])
                        m_u = Sc(se.activation(u_tm[:, ub, :], x_sb[xb][:],
                                               AF.Identity, bias=nmr_t[:],
                                               scale=rstd_t[:]))
                        m_x_use_s[xb] = m_u
                        m_x_use_v[xb] = cV.n
                        cS.wait(se, m_u)
                        for dc in range(8):
                            m_tr = uT[buf].dma(se.dma_start(
                                u_fm[buf][:, dc, s * 128:(s + 1) * 128],
                                u_tm[:, ub, dc * 128:(dc + 1) * 128],
                                transpose=True))
                    with contextlib.ExitStack() as ps:
                        qp_ps = [ps.enter_context(nc.psum_tensor(U("qp_ps"),
                                                                 [128, 512],
                                                                 F32))
                                 for i in range(2)]
                        kp_ps = ps.enter_context(nc.psum_tensor(U("kp_ps"),
                                                                [128, 4, PD],
                                                                F32))
                        uT[buf].wait(te, m_tr)
                        if bt >= 1:
                            cV.wait(te, m_kvadd_done)
                            cV.wait(te, m_z)
                        if bt == 0:
                            cS.wait(te, m_mqk_s)
                        for pc in range(2):
                            for dc in range(8):
                                m_qpmm = MM(qp_ps[pc][:],
                                            mq_sb[:, dc, pc * 128:(pc + 1) * 128],
                                            u_fm[buf][:, dc, :],
                                            start=(dc == 0), stop=(dc == 7),
                                            tag=(dc == 7 and pc == 1))
                        for s in range(4):
                            for dc in range(8):
                                m_qpkp = MM(kp_ps[:, s, :],
                                            u_fm[buf][:, dc, s * 128:(s + 1) * 128],
                                            mk_sb[:, dc, :],
                                            start=(dc == 0), stop=(dc == 7),
                                            tag=(s == 3 and dc == 7))
                        cP.wait(ve, m_qpmm)
                        cP.wait(ve, m_prev_tile_pe)
                        qpv0 = qp_ps[0][:]
                        qpv1 = qp_ps[1][:]
                        V(ve.tensor_scalar(el_t[:, 0:512], qpv0, 0.0, None,
                                           ALU.min))
                        V(ve.tensor_scalar(el_t[:, 512:1024], qpv1, 0.0, None,
                                           ALU.min))
                        V(ve.tensor_scalar(el_r[:, 0:512], qpv0, 0.0, None,
                                           ALU.max))
                        m_e1 = V(ve.tensor_scalar(el_r[:, 512:1024], qpv1, 0.0,
                                                  None, ALU.max))
                        cV.wait(se, m_e1)
                        m_e2 = Sc(se.activation(el_e[:], el_t[:], AF.Exp))
                        cS.wait(ve, m_e2)
                        m_qp = 0
                        for pc in range(2):
                            m_qp = V(ve.scalar_tensor_tensor(
                                qp_fm[:, pc, bt * 512:(bt + 1) * 512],
                                el_r[:, pc * 512:(pc + 1) * 512], 0.0,
                                el_e[:, pc * 512:(pc + 1) * 512], ALU.add,
                                ALU.add))
                        m_qp_ready[bt] = m_qp
                        cP.wait(ve, m_qpkp)
                        kpv = kp_ps[:].rearrange("p s f -> p (s f)")
                        cV.wait(ve, m_qp)
                        V(ve.tensor_scalar(el_t[:], kpv, 0.0, None, ALU.min))
                        m_e3 = V(ve.tensor_scalar(el_r[:], kpv, 0.0, None,
                                                  ALU.max))
                        cV.wait(se, m_e3)
                        m_e4 = Sc(se.activation(el_e[:], el_t[:], AF.Exp))
                        cS.wait(ve, m_e4)
                        m_kp = V(ve.scalar_tensor_tensor(
                            kp_tm[:].rearrange("p s f -> p (s f)"),
                            el_r[:], 0.0, el_e[:], ALU.add, ALU.add))
                    with contextlib.ExitStack() as ps:
                        zs_ps = ps.enter_context(nc.psum_tensor(U("zs_ps"),
                                                                [128, 4], F32))
                        cV.wait(te, m_qp_ready[bt])
                        for s in range(4):
                            for pc in range(2):
                                m_zmm = MM(
                                    zs_ps[:, s:s + 1],
                                    qp_fm[:, pc,
                                          bt * 512 + s * 128:
                                          bt * 512 + (s + 1) * 128],
                                    ones_bf[:, 0:1], start=(pc == 0),
                                    stop=(pc == 1),
                                    tag=(s == 3 and pc == 1))
                        cP.wait(ve, m_zmm)
                        cS.wait(ve, m_e4)
                        m_zt2 = V(ve.tensor_scalar(el_t[:, 0:4], zs_ps[:],
                                                   EPS_Z, None, ALU.add))
                        cV.wait(ve, m_zt2)
                        m_z = V(ve.reciprocal(z_sb[:, bt * 4:(bt + 1) * 4],
                                              el_t[:, 0:4]))
                        m_z_ready[bt] = m_z
                    with contextlib.ExitStack() as ps:
                        v_ps = [ps.enter_context(nc.psum_tensor(U("v_ps"),
                                                                [128, D], F32))
                                for i in range(2)]
                        kv_ps = ps.enter_context(nc.psum_tensor(U("kv_ps"),
                                                                [128, 2, D],
                                                                F32))
                        cV.wait(te, m_z)
                        m_vev = [0, 0]
                        for s in range(4):
                            r = s % 2
                            cS.wait(te, m_vev[r])
                            for dc in range(8):
                                for h in range(2):
                                    m_vmm = MM(
                                        v_ps[r][:, h * 512:(h + 1) * 512],
                                        u_fm[buf][:, dc, s * 128:(s + 1) * 128],
                                        wv_sb[:, dc, h * 512:(h + 1) * 512],
                                        start=(dc == 0), stop=(dc == 7),
                                        tag=(dc == 7 and h == 1))
                            cP.wait(se, m_vmm)
                            m_vev[r] = Sc(se.activation(v_tm[buf][:, s, :],
                                                        v_ps[r][:], AF.Copy))
                            cS.wait(te, m_vev[r])
                            cV.wait(te, m_kp)
                            for pc in range(2):
                                for h in range(2):
                                    m_kvlast = MM(
                                        kv_ps[:, pc, h * 512:(h + 1) * 512],
                                        kp_tm[:, s, pc * 128:(pc + 1) * 128],
                                        v_tm[buf][:, s, h * 512:(h + 1) * 512],
                                        start=(s == 0), stop=(s == 3),
                                        tag=(s == 3 and pc == 1 and h == 1))
                        m_ufm_use[buf] = m_kvlast
                        m_prev_tile_pe = m_kvlast
                        cP.wait(ve, m_kvlast)
                        kvv = kv_ps[:].rearrange("p c d -> p (c d)")
                        kvs = kv_sb[:].rearrange("p c d -> p (c d)")
                        if bt % 4 == 0:
                            m_kvadd_done = V(ve.tensor_copy(kvs, kvv))
                        else:
                            cV.wait(ve, m_kvadd_done)
                            m_kvadd_done = V(ve.tensor_tensor(kvs, kvs, kvv,
                                                              ALU.add))
                        if bt % 4 == 3:
                            cV.wait(ve, m_kvadd_done)
                            m_kv_ready[seq] = V(ve.tensor_copy(
                                kv_bf[:, seq, :, :].rearrange(
                                    "p c d -> p (c d)"), kvs))

            mB_v, mB_s, mB_p = cV.n, cS.n, cP.n
            for eng in (sy, se, ve, ge):
                cV.wait(eng, mB_v)
                cS.wait(eng, mB_s)
                cP.wait(eng, mB_p)
                for qq in (xs1[0], xs1[1], uT[0], uT[1], qZ, qWI):
                    qq.wait(eng)

            p2 = contextlib.ExitStack()
            with p2:
                fn_res = p2.enter_context(nc.sbuf_tensor("fn_res",
                                                         [128, NT, D], BF16))
                x2_sb = [p2.enter_context(nc.sbuf_tensor(f"x2_sb{i}",
                                                         [128, D], F32))
                         for i in range(2)]
                ao_sb = p2.enter_context(nc.sbuf_tensor("ao_sb", [128, D],
                                                        F32))
                t2_sb = p2.enter_context(nc.sbuf_tensor("t2_sb", [128, D],
                                                        F32))
                x1_sb = [p2.enter_context(nc.sbuf_tensor(f"x1_sb{i}",
                                                         [128, D], F32))
                         for i in range(2)]
                fe_sb = p2.enter_context(nc.sbuf_tensor("fe_sb", [128, D],
                                                        F32))
                fnT_sb = [p2.enter_context(nc.sbuf_tensor(f"fnT_sb{i}",
                                                          [128, 8, 128],
                                                          BF16))
                          for i in range(2)]
                sc_sb = p2.enter_context(nc.sbuf_tensor("sc_sb", [128, E],
                                                        F32))
                sc2_sb = p2.enter_context(nc.sbuf_tensor("sc2_sb", [128, E],
                                                         F32))
                mx8 = p2.enter_context(nc.sbuf_tensor("mx8", [128, 8], F32))
                mi8 = p2.enter_context(nc.sbuf_tensor("mi8", [128, 8], U32))
                midx_f = p2.enter_context(nc.sbuf_tensor("midx_f", [128, 1],
                                                         F32))
                oh_bf = p2.enter_context(nc.sbuf_tensor("oh_bf", [128, 8],
                                                        BF16))
                ofsB = p2.enter_context(nc.sbuf_tensor("ofsB", [128, 8], F32))
                t1_sb = p2.enter_context(nc.sbuf_tensor("t1_sb", [128, 8],
                                                        F32))
                ea_sb = p2.enter_context(nc.sbuf_tensor("ea_sb", [128, 1],
                                                        F32))
                on1 = p2.enter_context(nc.sbuf_tensor("on1", [128, 1], F32))
                rstd2f = p2.enter_context(nc.sbuf_tensor("rstd2f", [128, 1],
                                                         F32))
                nmr2f = p2.enter_context(nc.sbuf_tensor("nmr2f", [128, 1],
                                                        F32))
                mu2f = p2.enter_context(nc.sbuf_tensor("mu2f", [128, 1], F32))
                ir2f = p2.enter_context(nc.sbuf_tensor("ir2f", [128, 1], F32))
                r1f = p2.enter_context(nc.sbuf_tensor("r1f", [128, 1], F32))
                qbf = p2.enter_context(nc.sbuf_tensor("qbf", [128, 1], F32))
                lnq = p2.enter_context(nc.sbuf_tensor("lnq", [128, 1], F32))
                n_col_sb = p2.enter_context(nc.sbuf_tensor("n_col_sb", [8, 1],
                                                           F32))
                cmp8 = p2.enter_context(nc.sbuf_tensor("cmp8", [8, 8], F32))
                r8 = p2.enter_context(nc.sbuf_tensor("r8", [8, 1], F32))
                c_al_sb = p2.enter_context(nc.sbuf_tensor("c_al_sb", [8, 1],
                                                          F32))
                B_col_sb = p2.enter_context(nc.sbuf_tensor("B_col_sb", [8, 1],
                                                           F32))
                cmp2_sb = p2.enter_context(nc.sbuf_tensor("cmp2_sb", [8, NST],
                                                          F32))
                st_e_sb = p2.enter_context(nc.sbuf_tensor("st_e_sb", [1, NST],
                                                          F32))
                st_eB_sb = p2.enter_context(nc.sbuf_tensor("st_eB_sb",
                                                           [128, NST], F32))
                Brow32_sb = p2.enter_context(nc.sbuf_tensor("Brow32_sb",
                                                            [1, 256], F32))
                prod_sb = p2.enter_context(nc.sbuf_tensor("prod_sb",
                                                          [128, NT, E], F32))
                red_sb = p2.enter_context(nc.sbuf_tensor("red_sb", [128, NT],
                                                         F32))
                destf_sb = p2.enter_context(nc.sbuf_tensor("destf_sb",
                                                           [128, NT], F32))
                ohmat_sb = p2.enter_context(nc.sbuf_tensor("ohmat_sb",
                                                           [8, NST], F32))
                b2all_sb = p2.enter_context(nc.sbuf_tensor("b2all_sb",
                                                           [NST, D], F32))
                w1off_f = p2.enter_context(nc.sbuf_tensor("w1off_f",
                                                          [128, 8, NST], F32))
                w2off_f = p2.enter_context(nc.sbuf_tensor("w2off_f",
                                                          [128, 32, NST],
                                                          F32))
                imod8 = p2.enter_context(nc.sbuf_tensor("imod8", [8, 256],
                                                        F32))
                I8rep = p2.enter_context(nc.sbuf_tensor("I8rep", [8, 256],
                                                        F32))
                G_pm = p2.enter_context(nc.sbuf_tensor("G_pm", [E, D], F32))
                BT_pm = p2.enter_context(nc.sbuf_tensor("BT_pm", [E, D], F32))
                B1_pm = p2.enter_context(nc.sbuf_tensor("B1_pm", [E, FF],
                                                        F32))
                B2_pm = p2.enter_context(nc.sbuf_tensor("B2_pm", [E, D], F32))

                qG.dma(ge.dma_start(G_pm[:], gam_d))
                qG.dma(ge.dma_start(BT_pm[:], bet_d))
                qG.dma(ge.dma_start(B1_pm[:], b1r_d))
                qG.dma(ge.dma_start(B2_pm[:], b2r_d))
                m_gb = qG.n
                m_imod = cG.inc(ge.iota(
                    imod8[:].rearrange("p (a b) -> p a b", b=8),
                    [[0, 32], [1, 8]], base=0, channel_multiplier=0,
                    allow_small_or_imprecise_dtypes=True))
                cG.wait(ve, m_imod)
                m_i8rep = V(ve.tensor_scalar(I8rep[:], imod8[:],
                                             iota_p[0:8, 0:1], None,
                                             ALU.is_equal))

                with contextlib.ExitStack() as ps:
                    num_ps = [ps.enter_context(nc.psum_tensor(U("num_ps"),
                                                              [128, D], F32))
                              for i in range(2)]
                    sc_psF = ps.enter_context(nc.psum_tensor("sc_ps",
                                                             [128, 512], F32))
                    pos_psF = ps.enter_context(nc.psum_tensor("pos_ps",
                                                              [128, 512],
                                                              F32))
                    cnt_psF = ps.enter_context(nc.psum_tensor("cnt_ps",
                                                              [1, 512], F32))
                    baseB_psF = ps.enter_context(nc.psum_tensor("baseB_ps",
                                                                [128, 512],
                                                                F32))
                    sc_ps = sc_psF[:, 0:E]
                    pos_ps = pos_psF[:, 0:E]
                    cnt_ps = cnt_psF[:, 0:E]
                    baseB_ps = baseB_psF[:, 0:E]

                    m_x2_use = [0, 0]
                    m_x1_use_s = [0, 0]
                    m_x1_st = [0, 0]
                    m_fnT_use = [0, 0]
                    m_num_ev = [0, 0]
                    m_base_ev = 0
                    m_base_ev_is_g = True
                    m_sc_ev = 0
                    m_ofs_used = 0
                    m_pos_used = 0
                    m_pos_prev = 0

                    def load_x2(t, b):
                        cV.wait(ge, m_x2_use[b])
                        return xs2[b].dma(ge.dma_start(
                            x2_sb[b][:], x_d[t * 128:(t + 1) * 128, :]))

                    m_x2_ld = [0] * NT
                    m_x2_ld[0] = load_x2(0, 0)

                    for t in range(NT):
                        b = t % 2
                        seq = t // 16
                        if t + 1 < NT:
                            m_x2_ld[t + 1] = load_x2(t + 1, (t + 1) % 2)
                        if t == 0:
                            cV.wait(te, m_kv_ready[1])
                        cV.wait(te, m_kv_ready[seq])
                        cS.wait(te, m_num_ev[b])
                        for pc in range(2):
                            for h in range(2):
                                m_num = MM(num_ps[b][:, h * 512:(h + 1) * 512],
                                           qp_fm[:, pc, t * 128:(t + 1) * 128],
                                           kv_bf[:, seq, pc,
                                                 h * 512:(h + 1) * 512],
                                           start=(pc == 0), stop=(pc == 1),
                                           tag=(pc == 1 and h == 1))
                        cP.wait(se, m_num)
                        cV.wait(se, m_z_ready[t // 4])
                        m_ao = Sc(se.activation(ao_sb[:], num_ps[b][:],
                                                AF.Copy,
                                                scale=z_sb[:, t:t + 1]))
                        m_num_ev[b] = m_ao
                        cS.wait(ve, m_ao)
                        m_rstd2, m_nmr2 = stats(ao_sb)
                        cV.wait(se, m_nmr2)
                        m_t2 = Sc(se.activation(t2_sb[:], ao_sb[:],
                                                AF.Identity, bias=nmr_t[:],
                                                scale=rstd_t[:]))
                        xs2[b].wait(ve, m_x2_ld[t])
                        cS.wait(ve, m_t2)
                        qX1[b].wait(ve, m_x1_st[b])
                        cS.wait(ve, m_x1_use_s[b])
                        m_x1 = V(ve.tensor_tensor(x1_sb[b][:], x2_sb[b][:],
                                                  t2_sb[:], ALU.add))
                        m_x2_use[b] = m_x1
                        cV.wait(ge, m_x1)
                        m_x1_st[b] = qX1[b].dma(
                            ge.dma_start(x1_d[t * 128:(t + 1) * 128, :],
                                         x1_sb[b][:]))
                        m_rstd3, m_nmr3 = stats(x1_sb[b],
                                                rstd_out=s3_sb[:, t:t + 1],
                                                nmr_out=b3_sb[:, t:t + 1])
                        cV.wait(se, m_nmr3)
                        m_fe = Sc(se.activation(fe_sb[:], x1_sb[b][:],
                                                AF.Identity,
                                                bias=b3_sb[:, t:t + 1],
                                                scale=s3_sb[:, t:t + 1]))
                        m_x1_use_s[b] = m_fe
                        m_r1 = V(ve.reciprocal(r1f[:], varp[:]))
                        cV.wait(ve, m_r1)
                        m_qb = V(ve.tensor_scalar(qbf[:], mv2[:, 1:2],
                                                  r1f[:], EPS_LN,
                                                  ALU.mult, ALU.add))
                        cV.wait(se, m_qb)
                        m_lq = Sc(se.activation(lnq[:], qbf[:], AF.Ln))
                        cS.wait(se, m_lq)
                        Sc(se.activation(rstd2f[:], lnq[:], AF.Exp,
                                         scale=-0.5))
                        m_ir = Sc(se.activation(ir2f[:], lnq[:], AF.Exp,
                                                scale=0.5))
                        cS.wait(se, m_ir)
                        m_fn = Sc(se.activation(fn_res[:, t, :], fe_sb[:],
                                                AF.Identity,
                                                scale=rstd2f[:]))
                        cS.wait(sy, m_fn)
                        cP.wait(sy, m_fnT_use[b])
                        m_fnT = 0
                        for dc in range(8):
                            m_fnT = fT[b].dma(sy.dma_start(
                                fnT_sb[b][:, dc, :],
                                fn_res[:, t, dc * 128:(dc + 1) * 128],
                                transpose=True))
                        fT[b].wait(te, m_fnT)
                        cV.wait(te, m_sc_ev)
                        for dc in range(8):
                            m_sc_last = MM(sc_ps, fnT_sb[b][:, dc, :],
                                           ct_sb[:, dc, :],
                                           start=(dc == 0), stop=(dc == 7),
                                           tag=(dc == 7))
                        m_fnT_use[b] = m_sc_last
                        cP.wait(ve, m_sc_last)
                        cS.wait(ve, m_ir)
                        m_sc_ev = V(ve.tensor_scalar(sc_sb[:], sc_ps, ir2f[:],
                                                     None, ALU.mult))
                        cV.wait(ve, m_sc_ev)
                        m_mx = V(ve.max(mx8[:], sc_sb[:]))
                        cV.wait(ve, m_mx)
                        m_mi = V(ve.max_index(mi8[:], mx8[:], sc_sb[:]))
                        cV.wait(se, m_mx)
                        m_ea = Sc(se.activation(ea_sb[:], mx8[:, 0:1], AF.Exp,
                                                scale=-1.0))
                        cS.wait(ve, m_ea)
                        m_o1 = V(ve.tensor_scalar(on1[:], ea_sb[:], 1.0, None,
                                                  ALU.add))
                        cV.wait(ve, m_o1)
                        m_al = V(ve.reciprocal(alpha_sb[:, t:t + 1], on1[:]))
                        cV.wait(ve, m_mi)
                        cP.wait(ve, m_pos_prev)
                        m_mif = V(ve.tensor_copy(midx_f[:], mi8[:, 0:1]))
                        cV.wait(ve, m_mif)
                        m_oh = V(ve.tensor_scalar(oh_all[:, t, :], iota8[:],
                                                  midx_f[:], None,
                                                  ALU.is_equal))
                        cV.wait(ve, m_oh)
                        m_ohb = V(ve.tensor_copy(oh_bf[:], oh_all[:, t, :]))
                        if m_base_ev_is_g:
                            cG.wait(te, m_init_g)
                        else:
                            cV.wait(te, m_base_ev)
                        cV.wait(te, m_ofs_used)
                        m_bb = MM(baseB_ps, ones_f[:], base_row[:], start=True,
                                  stop=True, tag=True)
                        cP.wait(ve, m_bb)
                        m_ofs = V(ve.tensor_tensor(ofsB[:], baseB_ps,
                                                   mione[:], ALU.add))
                        m_ofs_used = m_ofs
                        cV.wait(te, m_ohb)
                        cV.wait(te, m_pos_used)
                        MM(pos_ps, tri_bf[:], oh_bf[:], start=True, stop=True,
                           tag=False)
                        m_pos = MM(cnt_ps, ones_bf[:, 0:1], oh_bf[:],
                                   start=True, stop=True, tag=True)
                        m_pos_prev = m_pos
                        cP.wait(ve, m_pos)
                        cV.wait(ve)
                        m_base_ev = V(ve.tensor_tensor(base_row[:],
                                                       base_row[:], cnt_ps,
                                                       ALU.add))
                        m_base_ev_is_g = False
                        cV.wait(ve, m_ofs)
                        m_t1 = V(ve.tensor_tensor(t1_sb[:], pos_ps, ofsB[:],
                                                  ALU.add))
                        m_pos_used = m_t1
                        cV.wait(ve, m_t1)
                        m_t1b = V(ve.tensor_tensor(t1_sb[:], t1_sb[:],
                                                   oh_all[:, t, :], ALU.mult))
                        cV.wait(ve, m_t1b)
                        V(ve.reduce_sum(pos_all[:, t:t + 1], t1_sb[:], AX.X))

                    m_fn_all = cS.n

                with contextlib.ExitStack() as ps:
                    ps_cnt = ps.enter_context(nc.psum_tensor("f_cnt", [8, 1],
                                                             F32))
                    ps_pre = ps.enter_context(nc.psum_tensor("f_pre", [8, 1],
                                                             F32))
                    ps_ste = ps.enter_context(nc.psum_tensor("f_ste",
                                                             [1, NST], F32))
                    ps_steB = ps.enter_context(nc.psum_tensor("f_steB",
                                                              [128, NST],
                                                              F32))
                    ps_brd = ps.enter_context(nc.psum_tensor("f_brd",
                                                             [1, 256], F32))
                    ps_bB = ps.enter_context(nc.psum_tensor("f_bB",
                                                            [128, 256], F32))
                    mRF_v, mRF_s = cV.n, cS.n
                    cV.wait(te, mRF_v)
                    cS.wait(te, mRF_s)
                    cG.wait(te)
                    cV.wait(ve, mRF_v)
                    cS.wait(ve, mRF_s)
                    m1 = MM(ps_cnt[:], base_row[:], ones_f[0:1, 0:1],
                            start=True, stop=True, tag=True)
                    cP.wait(ve, m1)
                    e1 = V(ve.tensor_copy(n_col_sb[:], ps_cnt[:]))
                    cV.wait(ve, e1)
                    e2 = V(ve.tensor_scalar(cmp8[:], thrT[:, 0:8],
                                            n_col_sb[:], None, ALU.is_lt))
                    cV.wait(ve, e2)
                    e3 = V(ve.reduce_sum(r8[:], cmp8[:], AX.X))
                    cV.wait(ve, e3)
                    e4 = V(ve.tensor_scalar(c_al_sb[:], r8[:], float(A), None,
                                            ALU.mult))
                    cV.wait(te, e4)
                    m2 = MM(ps_pre[:], sU8[:], c_al_sb[:], start=True,
                            stop=True, tag=True)
                    cP.wait(ve, m2)
                    e5 = V(ve.tensor_copy(B_col_sb[:], ps_pre[:]))
                    cV.wait(ve, e5)
                    e6 = V(ve.tensor_scalar(cmp2_sb[:], thrT[:, 0:NST],
                                            B_col_sb[:], None, ALU.is_ge))
                    cV.wait(te, e6)
                    m3 = MM(ps_ste[:], ones8[:], cmp2_sb[:], start=True,
                            stop=True, tag=True)
                    cP.wait(ve, m3)
                    e7 = V(ve.tensor_scalar(st_e_sb[:], ps_ste[:], -1.0,
                                            None, ALU.add))
                    cV.wait(te, e7)
                    m4 = MM(ps_steB[:], ones_f[:], st_e_sb[:],
                            start=True, stop=True, tag=True)
                    cP.wait(ve, m4)
                    e8 = V(ve.tensor_copy(st_eB_sb[:], ps_steB[:]))
                    m5 = MM(ps_brd[:], B_col_sb[:], I8rep[:], start=True,
                            stop=True, tag=True)
                    cP.wait(ve, m5)
                    e9 = V(ve.tensor_copy(Brow32_sb[:], ps_brd[:]))
                    cV.wait(te, e9)
                    m6 = MM(ps_bB[:], ones_f[:], Brow32_sb[:],
                            start=True, stop=True, tag=True)
                    cP.wait(ve, m6)
                    e10 = V(ve.tensor_tensor(
                        prod_sb[:].rearrange("p t e -> p (t e)"),
                        oh_all[:].rearrange("p t e -> p (t e)"),
                        ps_bB[:], ALU.mult))
                    cV.wait(ve, e10)
                    e11 = V(ve.reduce_sum(red_sb[:], prod_sb[:], AX.X))
                    cV.wait(ve, e11)
                    e12 = V(ve.tensor_tensor(destf_sb[:], red_sb[:],
                                             pos_all[:], ALU.add))
                    cV.wait(ve, e12)
                    m_dest_all = V(ve.tensor_copy(dest_sb[:], destf_sb[:]))
                    cV.wait(ve, e8)
                    eo = 0
                    for dc in range(8):
                        eo = V(ve.tensor_scalar(w1off_f[:, dc, :],
                                                st_eB_sb[:], float(D),
                                                iota_pdc[:, dc:dc + 1],
                                                ALU.mult, ALU.add))
                    for fc in range(32):
                        eo = V(ve.tensor_scalar(w2off_f[:, fc, :],
                                                st_eB_sb[:], float(FF),
                                                iota_pfc[:, fc:fc + 1],
                                                ALU.mult, ALU.add))
                    cV.wait(ve, eo)
                    V(ve.tensor_copy(
                        w1off_u[:].rearrange("p a b -> p (a b)"),
                        w1off_f[:].rearrange("p a b -> p (a b)")))
                    m_offs_ready = V(ve.tensor_copy(
                        w2off_u[:].rearrange("p a b -> p (a b)"),
                        w2off_f[:].rearrange("p a b -> p (a b)")))
                    m_f1_done = cV.n
                with contextlib.ExitStack() as ps:
                    ps_ohm = ps.enter_context(nc.psum_tensor("f_ohm",
                                                             [8, NST], F32))
                    ps_b2 = [ps.enter_context(nc.psum_tensor(f"f_b2{h}",
                                                             [NST, 512], F32))
                             for h in range(2)]
                    ps_sel = [ps.enter_context(nc.psum_tensor(f"f_sel{i}",
                                                              [128, NST],
                                                              F32))
                              for i in range(2)]
                    cV.wait(te, m_f1_done)
                    m7 = MM(ps_ohm[:], D8[:], cmp2_sb[:], start=True,
                            stop=True, tag=True)
                    cP.wait(ve, m7)
                    e14 = V(ve.tensor_copy(ohmat_sb[:], ps_ohm[:]))
                    qG.wait(te, m_gb)
                    cV.wait(te, e14)
                    for h in range(2):
                        m8 = MM(ps_b2[h][:], ohmat_sb[:],
                                B2_pm[:, h * 512:(h + 1) * 512], start=True,
                                stop=True, tag=(h == 1))
                    cP.wait(ve, m8)
                    V(ve.tensor_copy(b2all_sb[:, 0:512], ps_b2[0][:]))
                    e15 = V(ve.tensor_copy(b2all_sb[:, 512:1024],
                                           ps_b2[1][:]))
                    cV.wait(sy, e15)
                    qB.dma(sy.dma_start(b2scr_2d, b2all_sb[:]))
                    m_sel_ev = [0, 0]
                    k = 0
                    for pm, dall, nch in ((G_pm, g_all, 8), (BT_pm, bt_all, 8),
                                          (B1_pm, b1_all, 32)):
                        for c in range(nch):
                            r = k % 2
                            cV.wait(te, m_sel_ev[r])
                            msel = MM(ps_sel[r][:],
                                      pm[:, c * 128:(c + 1) * 128],
                                      ohmat_sb[:], start=True, stop=True,
                                      tag=True)
                            cP.wait(ve, msel)
                            m_sel_ev[r] = V(ve.tensor_copy(dall[:, c, :],
                                                           ps_sel[r][:]))
                            k += 1
                    m_gall = cV.n

                cV.wait(ge, m_dest_all)
                cS.wait(ge, m_fn_all)
                qZ.wait(ge)
                for t in range(NT):
                    qIs[t % 2].dma(ge.indirect_dma_start(
                        fn_tbl,
                        bass.IndirectOffsetOnAxis(ap=dest_sb[:, t:t + 1],
                                                  axis=0),
                        fn_res[:, t, :], None, bounds_check=None))

        mC_v, mC_s, mC_p = cV.n, cS.n, cP.n
        for eng in (te, ve, se, sy, ge):
            cV.wait(eng, mC_v)
            cS.wait(eng, mC_s)
            cP.wait(eng, mC_p)
            for qq in (qIs[0], qIs[1], qX1[0], qX1[1], xs2[0], xs2[1],
                       fT[0], fT[1], qB):
                qq.wait(eng)

        ff = contextlib.ExitStack()
        with ff:
            w1_sb = ff.enter_context(nc.sbuf_tensor("w1_sb", [128, 8, FF],
                                                    BF16))
            w2_sb = ff.enter_context(nc.sbuf_tensor("w2_sb", [128, 32, D],
                                                    BF16))
            hr_sb = ff.enter_context(nc.sbuf_tensor("hr_sb", [128, 8, 512],
                                                    BF16))
            h_sb = ff.enter_context(nc.sbuf_tensor("h_sb", [128, 8, 512],
                                                   BF16))
            a1r = ff.enter_context(nc.sbuf_tensor("a1r", [128, 32, 512],
                                                  BF16))
            y2_sb = ff.enter_context(nc.sbuf_tensor("y2_sb", [128, D], F32))
            b2B = ff.enter_context(nc.sbuf_tensor("b2B", [128, D], F32))
            b2row = [ff.enter_context(nc.sbuf_tensor(f"b2row{i}", [1, D],
                                                     F32))
                     for i in range(2)]

            with contextlib.ExitStack() as ps:
                a1_ps = [ps.enter_context(nc.psum_tensor(U("a1_ps"),
                                                         [128, 512], F32))
                         for i in range(2)]
                y2_ps = [ps.enter_context(nc.psum_tensor(U("y2_ps"), [128, D],
                                                         F32))
                         for i in range(2)]
                b2B_ps = ps.enter_context(nc.psum_tensor("b2B_ps", [128, D],
                                                         F32))

                m_w1_use = 0
                m_w2_use = 0
                m_h_use = 0
                m_hr_free = 0
                m_a1_ev_s = [0, 0]
                m_a1_ev_v = [0, 0]
                m_y2_ev = [0, 0]
                m_y2_st = 0
                m_b2B_use = 0
                m_b2B = 0
                m_b2mm_l = [0, 0]
                m_b2ld = [0, 0]

                for s in range(NST):
                    cP.wait(ge, m_w1_use)
                    if s == 0:
                        cV.wait(ge, m_offs_ready)
                    m_w1_ld = 0
                    for dc in range(8):
                        m_w1_ld = qW1.dma(ge.indirect_dma_start(
                            w1_sb[:, dc, :], None, w1_lin,
                            bass.IndirectOffsetOnAxis(
                                ap=w1off_u[:, dc, s:s + 1], axis=0),
                            bounds_check=None))
                    cP.wait(ge, m_w2_use)
                    m_w2_ld = 0
                    for fc in range(32):
                        m_w2_ld = qW2.dma(ge.indirect_dma_start(
                            w2_sb[:, fc, :], None, w2_lin,
                            bass.IndirectOffsetOnAxis(
                                ap=w2off_u[:, fc, s:s + 1], axis=0),
                            bounds_check=None))
                    bb = s % 2
                    cP.wait(sy, m_b2mm_l[bb])
                    if s == 0:
                        qB.wait(sy)
                    m_b2ld[bb] = qB.dma(sy.dma_start(b2row[bb][:],
                                                     b2scr_2d[s:s + 1, :]))
                    cV.wait(te, m_b2B_use)
                    qB.wait(te, m_b2ld[bb])
                    for h in range(2):
                        m_b2mm = MM(b2B_ps[:, h * 512:(h + 1) * 512],
                                    ones_f[:],
                                    b2row[bb][:, h * 512:(h + 1) * 512],
                                    start=True, stop=True, tag=(h == 1))
                    m_b2mm_l[bb] = m_b2mm
                    cP.wait(ve, m_b2mm)
                    m_b2B = V(ve.tensor_copy(b2B[:], b2B_ps[:]))
                    cP.wait(se, m_h_use)
                    if s == 0:
                        cV.wait(se, m_gall)
                    cS.wait(se, m_hr_free)
                    m_htr = 0
                    for dc in range(8):
                        m_htr = hT.dma(se.dma_start(
                            hr_sb[:, dc, :],
                            fn_tbl[s * 512:(s + 1) * 512,
                                   dc * 128:(dc + 1) * 128],
                            transpose=True))
                    hT.wait(se, m_htr)
                    m_h = 0
                    for dc in range(8):
                        m_h = Sc(se.activation(h_sb[:, dc, :],
                                               hr_sb[:, dc, :],
                                               AF.Identity,
                                               bias=bt_all[:, dc, s:s + 1],
                                               scale=g_all[:, dc, s:s + 1]))
                    m_hr_free = m_h
                    qW1.wait(te, m_w1_ld)
                    cS.wait(te, m_h)
                    for fc in range(32):
                        ar = fc % 2
                        cS.wait(te, m_a1_ev_s[ar])
                        cV.wait(te, m_a1_ev_v[ar])
                        for dc in range(8):
                            m_a1mm = MM(a1_ps[ar][:],
                                        w1_sb[:, dc, fc * 128:(fc + 1) * 128],
                                        h_sb[:, dc, :],
                                        start=(dc == 0), stop=(dc == 7),
                                        tag=(dc == 7))
                        if fc % 2 == 0:
                            cP.wait(se, m_a1mm)
                            m_a1_ev_s[ar] = Sc(se.activation(
                                a1r[:, fc, :], a1_ps[ar][:], AF.Relu,
                                bias=b1_all[:, fc, s:s + 1]))
                        else:
                            cP.wait(ve, m_a1mm)
                            m_a1_ev_v[ar] = V(ve.tensor_scalar(
                                a1r[:, fc, :], a1_ps[ar][:],
                                b1_all[:, fc, s:s + 1], 0.0,
                                ALU.add, ALU.max))
                    m_w1_use = cP.n
                    m_h_use = cP.n
                    qW2.wait(te, m_w2_ld)
                    m_s1s = cS.n
                    m_s1v = cV.n
                    cS.wait(te, m_s1s)
                    cV.wait(te, m_s1v)
                    for sub in range(4):
                        yr = sub % 2
                        cV.wait(te, m_y2_ev[yr])
                        for fc in range(32):
                            for h in range(2):
                                m_y2mm = MM(
                                    y2_ps[yr][:, h * 512:(h + 1) * 512],
                                    a1r[:, fc, sub * 128:(sub + 1) * 128],
                                    w2_sb[:, fc, h * 512:(h + 1) * 512],
                                    start=(fc == 0), stop=(fc == 31),
                                    tag=(fc == 31 and h == 1))
                        cP.wait(ve, m_y2mm)
                        cV.wait(ve, m_b2B)
                        qY2.wait(ve, m_y2_st)
                        m_y2_ev[yr] = V(ve.scalar_tensor_tensor(
                            y2_sb[:], y2_ps[yr][:], 0.0, b2B[:],
                            ALU.add, ALU.add))
                        cV.wait(sy, m_y2_ev[yr])
                        m_y2_st = qY2.dma(sy.dma_start(
                            y2_tbl[s * 512 + sub * 128:
                                   s * 512 + (sub + 1) * 128, :],
                            y2_sb[:]))
                    m_w2_use = cP.n
                    m_b2B_use = cV.n

        mD_v, mD_s, mD_p = cV.n, cS.n, cP.n
        for eng in (ge, sy, se, ve):
            cV.wait(eng, mD_v)
            cS.wait(eng, mD_s)
            cP.wait(eng, mD_p)
            for qq in (qY2, qW1, qW2, hT, qB):
                qq.wait(eng)

        cb = contextlib.ExitStack()
        with cb:
            y2g = [cb.enter_context(nc.sbuf_tensor(f"y2g{i}", [128, D], F32))
                   for i in range(2)]
            x1c = [cb.enter_context(nc.sbuf_tensor(f"x1c{i}", [128, D], F32))
                   for i in range(2)]
            fe2 = cb.enter_context(nc.sbuf_tensor("fe2", [128, D], F32))
            of_sb = cb.enter_context(nc.sbuf_tensor("of_sb", [128, D], F32))
            t5_sb = cb.enter_context(nc.sbuf_tensor("t5_sb", [128, D], F32))
            xo_sb = [cb.enter_context(nc.sbuf_tensor(f"xo_sb{i}", [128, D],
                                                     F32))
                     for i in range(2)]
            bn6b = cb.enter_context(nc.sbuf_tensor("bn6b", [128, 12], F32))
            mv2b = cb.enter_context(nc.sbuf_tensor("mv2b", [128, 2], F32))
            varb = cb.enter_context(nc.sbuf_tensor("varb", [128, 1], F32))
            lnvb = cb.enter_context(nc.sbuf_tensor("lnvb", [128, 1], F32))
            rstdb = cb.enter_context(nc.sbuf_tensor("rstdb", [128, 1], F32))
            nmrb = cb.enter_context(nc.sbuf_tensor("nmrb", [128, 1], F32))

            st5 = {"se": 0, "ve": 0}

            def stats5(src):
                cS.wait(ve, st5["se"])
                cV.wait(ve)
                V(ve.bn_stats(bn6b[:, 0:6], src[:, 0:512]))
                m1 = V(ve.bn_stats(bn6b[:, 6:12], src[:, 512:1024]))
                cV.wait(ve, m1)
                m2 = V(ve.bn_aggr(mv2b[:], bn6b[:]))
                cV.wait(ve, m2)
                m3 = V(ve.tensor_scalar(varb[:], mv2b[:, 1:2], EPS_LN, None,
                                        ALU.add))
                cV.wait(se, m3)
                m4 = Sc(se.activation(lnvb[:], varb[:], AF.Ln))
                cS.wait(se, m4)
                m5 = Sc(se.activation(rstdb[:], lnvb[:], AF.Exp, scale=-0.5))
                st5["se"] = m5
                cS.wait(ve, m5)
                m6 = V(ve.tensor_scalar(nmrb[:], mv2b[:, 0:1], rstdb[:], -1.0,
                                        ALU.mult, ALU.mult))
                st5["ve"] = m6
                return m5, m6

            m_y2g_use = [0, 0]
            m_x1c_use_s = [0, 0]
            m_x1c_use_v = [0, 0]
            m_xo_st = [0, 0]

            def load_cb(t, b):
                cV.wait(ge, m_y2g_use[b])
                mg = qIg[b].dma(ge.indirect_dma_start(
                    y2g[b][:], None,
                    y2_tbl,
                    bass.IndirectOffsetOnAxis(ap=dest_sb[:, t:t + 1], axis=0),
                    bounds_check=None))
                cS.wait(sy, m_x1c_use_s[b])
                cV.wait(sy, m_x1c_use_v[b])
                mx = qXC[b].dma(sy.dma_start(x1c[b][:],
                                             x1_d[t * 128:(t + 1) * 128, :]))
                return mg, mx

            m_g = [None] * NT
            m_g[0] = load_cb(0, 0)

            for t in range(NT):
                b = t % 2
                if t + 1 < NT:
                    m_g[t + 1] = load_cb(t + 1, (t + 1) % 2)
                mg, mx = m_g[t]
                qXC[b].wait(se, mx)
                m_fe2 = Sc(se.activation(fe2[:], x1c[b][:], AF.Identity,
                                         bias=b3_sb[:, t:t + 1],
                                         scale=s3_sb[:, t:t + 1]))
                qIg[b].wait(ve, mg)
                cS.wait(ve, m_fe2)
                m_of = V(ve.scalar_tensor_tensor(
                    of_sb[:], y2g[b][:], alpha_sb[:, t:t + 1], fe2[:],
                    ALU.mult, ALU.add))
                m_y2g_use[b] = m_of
                cV.wait(ve, m_of)
                m_r5, m_n5 = stats5(of_sb)
                cV.wait(se, m_n5)
                m_t5 = Sc(se.activation(t5_sb[:], of_sb[:], AF.Identity,
                                        bias=nmrb[:], scale=rstdb[:]))
                m_x1c_use_s[b] = m_t5
                cS.wait(ve, m_t5)
                qO[b].wait(ve, m_xo_st[b])
                m_x2o = V(ve.tensor_tensor(xo_sb[b][:], x1c[b][:], t5_sb[:],
                                           ALU.add))
                m_x1c_use_v[b] = m_x2o
                cV.wait(sy, m_x2o)
                m_xo_st[b] = qO[b].dma(sy.dma_start(
                    out_d[t * 128:(t + 1) * 128, :], xo_sb[b][:]))

        qO[0].wait(sy)
        qO[1].wait(sy)
        qIg[0].wait(ge)
        qIg[1].wait(ge)
    return nc


_NC_CACHE = {}


def _get_nc():
    key = 0
    if key not in _NC_CACHE:
        _NC_CACHE[key] = build()
    return _NC_CACHE[key]


def prep_in_maps(inputs):
    x = np.ascontiguousarray(np.asarray(inputs["L0_attn_x"], dtype=np.float32))
    wq = np.asarray(inputs["L0_attn_Wq"], dtype=np.float32)
    wk = np.asarray(inputs["L0_attn_Wk"], dtype=np.float32)
    wv = np.asarray(inputs["L0_attn_Wv"], dtype=np.float32)
    proj = np.ascontiguousarray(np.asarray(inputs["L0_attn_proj"],
                                           dtype=np.float32))
    cent = np.asarray(inputs["L0_ffn_centroids"], dtype=np.float32)
    gam = np.ascontiguousarray(np.asarray(inputs["L0_ffn_gamma"],
                                          dtype=np.float32))
    bet = np.ascontiguousarray(np.asarray(inputs["L0_ffn_beta"],
                                          dtype=np.float32))
    W1 = np.asarray(inputs["L0_ffn_W1"], dtype=np.float32)
    b1 = np.ascontiguousarray(np.asarray(inputs["L0_ffn_b1"],
                                         dtype=np.float32))
    W2 = np.asarray(inputs["L0_ffn_W2"], dtype=np.float32)
    b2 = np.ascontiguousarray(np.asarray(inputs["L0_ffn_b2"],
                                         dtype=np.float32))

    wqr = np.ascontiguousarray(wq)
    wkr = np.ascontiguousarray(wk)
    wvT = np.ascontiguousarray(wv.T)
    cT = np.ascontiguousarray(cent.T)
    w1T = np.ascontiguousarray(W1.transpose(0, 2, 1))
    w2T = np.ascontiguousarray(W2.transpose(0, 2, 1))

    in_maps = []
    for c in range(NCORE):
        xm = np.ascontiguousarray(x[2 * c:2 * c + 2].reshape(T, D))
        in_maps.append({
            "x": xm, "wqr": wqr, "wkr": wkr, "wvT": wvT, "proj": proj,
            "cT": cT, "gam": gam, "bet": bet, "b1r": b1, "b2r": b2,
            "w1T": w1T, "w2T": w2T,
        })
    return in_maps


def kernel(**inputs):
    in_maps = prep_in_maps(inputs)
    nc = _get_nc()
    res = run_bass_kernel_spmd(nc, in_maps, core_ids=list(range(NCORE)))
    outs = [res.results[c]["out"].reshape(2, 2048, D) for c in range(NCORE)]
    full = np.concatenate(outs, axis=0)
    return full
